# version 13
# baseline (speedup 1.0000x reference)
import numpy as np

import concourse.bass as bass
import concourse.mybir as mybir
from concourse.bass_utils import run_bass_kernel_spmd

F32 = mybir.dt.float32
BF16 = mybir.dt.bfloat16
I32 = mybir.dt.int32
U8 = mybir.dt.uint8
AL = mybir.AluOpType
AF = mybir.ActivationFunctionType
AX = mybir.AxisListType

NC = 8
BPC = 2
L = 2048
D = 1024
H = 16
HD = 64
WIN = 512
DI = 2816
KCH = D // 128
NBQ = 3 * D // 128
NBI = DI // 128
ST = 512
NST = L // ST
NQT = L // 128
RING = 6

MAGIC = 12582912.0
FMIN = float(np.finfo(np.float32).min)
QMAX = 127.0
ARENA_BYTES = 198000

_LIMIT_QT = None


ENGINES = ("pe", "dve", "act", "pool", "sync")


class _Op:
    __slots__ = ("eng", "fn", "deps", "idx", "inc", "kind", "waits", "lane")

    def __init__(self, eng, fn, deps, idx, kind):
        self.eng = eng
        self.fn = fn
        self.deps = deps
        self.idx = idx
        self.inc = False
        self.kind = kind
        self.waits = None
        self.lane = None


def _names(items):
    out = []
    for a in items:
        if a is None or isinstance(a, (int, float)):
            continue
        t = getattr(a, "tensor", None)
        if t is not None:
            nm = t.name
            if type(t).__name__.startswith("PSum"):
                try:
                    dts = mybir.dt.size(a.dtype)
                    off = int(a.offset) * dts
                    span = 1
                    pairs = [tuple(p) for p in a.ap][1:]
                    for st_, ct_ in pairs:
                        span += (int(ct_) - 1) * abs(int(st_))
                    b0 = off // 2048
                    b1 = (off + span * dts - 1) // 2048
                    for bb in range(b0, b1 + 1):
                        out.append(f"{nm}#b{bb}")
                    continue
                except Exception:
                    pass
            out.append(nm)
        elif hasattr(a, "name"):
            out.append(a.name)
    return out


class Sched:
    NLANES = 8

    def __init__(self):
        self.q = {e: [] for e in ENGINES}
        self.last_w = {}
        self.last_r = {}
        self.pending_barrier = {e: [] for e in ENGINES}
        self.dma_count = {e: 0 for e in ENGINES}

    def op(self, eng, fn, reads=(), writes=(), kind="c"):
        idx = len(self.q[eng])
        deps = {}

        def add(p):
            if p is None:
                return
            e2, i2 = p
            if e2 == eng and i2 >= idx:
                return
            if e2 == "pe" and eng == "pe":
                return
            o2 = self.q[e2][i2]
            if o2.kind == "d":
                key = (e2, "d", o2.lane)
            else:
                key = (e2, o2.kind)
            deps[key] = max(deps.get(key, -1), i2)

        rnames = _names(reads)
        wnames = _names(writes)
        for nm in rnames:
            add(self.last_w.get(nm))
        for nm in wnames:
            add(self.last_w.get(nm))
            for e2, i2 in self.last_r.get(nm, {}).items():
                add((e2, i2))
        for p in self.pending_barrier[eng]:
            add(p)
        self.pending_barrier[eng] = []
        o = _Op(eng, fn, list(deps.items()), idx, kind)
        if kind == "d":
            o.lane = self.dma_count[eng] % self.NLANES
            self.dma_count[eng] += 1
        self.q[eng].append(o)
        for nm in rnames:
            self.last_r.setdefault(nm, {})[eng] = idx
        for nm in wnames:
            self.last_w[nm] = (eng, idx)
            self.last_r[nm] = {}
        return o

    def barrier(self):
        marks = [(e, len(self.q[e]) - 1) for e in ENGINES if self.q[e]]
        for e in ENGINES:
            self.pending_barrier[e] = [(e2, i2) for e2, i2 in marks if e2 != e]

    def finalize(self):
        need = {e: set() for e in ENGINES}
        for e in ENGINES:
            for o in self.q[e]:
                for key, i2 in o.deps:
                    need[key[0]].add(i2)
        self.val = {}
        self.lane_final = {e: [0] * self.NLANES for e in ENGINES}
        for e in ENGINES:
            cnt = {"c": 0, "x": 0}
            lane_cnt = [0] * self.NLANES
            for o in self.q[e]:
                if o.kind == "d":
                    o.inc = True
                    lane_cnt[o.lane] += 16
                    self.val[(e, o.idx)] = lane_cnt[o.lane]
                elif o.kind == "x":
                    o.inc = True
                    cnt["x"] += 1
                    self.val[(e, o.idx)] = cnt["x"]
                else:
                    if o.idx in need[e]:
                        o.inc = True
                        cnt["c"] += 1
                    self.val[(e, o.idx)] = cnt["c"]
            self.lane_final[e] = lane_cnt
        nxt = {}
        for e in ENGINES:
            nv = {"c": None, "x": None}
            for o in reversed(self.q[e]):
                if o.inc and o.kind != "d":
                    nv[o.kind] = self.val[(e, o.idx)]
                nxt[(e, o.idx)] = dict(nv)
        for e in ENGINES:
            hw = {}
            for o in self.q[e]:
                fw = []
                if o.kind == "d":
                    v = self.val[(e, o.idx)] - 16
                    if v > 0:
                        key = (e, "d", o.lane)
                        if hw.get(key, 0) < v:
                            hw[key] = v
                            fw.append((key, v))
                for key, i2 in o.deps:
                    if len(key) == 3:
                        v = self.val[(key[0], i2)]
                    else:
                        v = nxt[(key[0], i2)][key[1]]
                        assert v is not None, (e, o.idx, key, i2)
                    if hw.get(key, 0) < v:
                        hw[key] = v
                        fw.append((key, v))
                o.waits = fw
        return self



def build_program():
    nc = bass.Bass("TRN2", target_bir_lowering=False)
    S = Sched()

    x_in = nc.dram_tensor("x_in", [BPC, L, D], F32, kind="ExternalInput").ap()
    c_in = nc.dram_tensor("c_in", [BPC, D], F32, kind="ExternalInput").ap()
    w_qkv = nc.dram_tensor("w_qkv", [D, 3 * D], F32, kind="ExternalInput").ap()
    w_out = nc.dram_tensor("w_out", [D, D], F32, kind="ExternalInput").ap()
    wmod = nc.dram_tensor("wmod", [D, 6 * D], F32, kind="ExternalInput").ap()
    w_gate = nc.dram_tensor("w_gate", [DI, D], F32, kind="ExternalInput").ap()
    w_up = nc.dram_tensor("w_up", [DI, D], F32, kind="ExternalInput").ap()
    w_down = nc.dram_tensor("w_down", [D, DI], F32, kind="ExternalInput").ap()
    clip_x = nc.dram_tensor("clip_x", [1], F32, kind="ExternalInput").ap()
    clip_dn = nc.dram_tensor("clip_dn", [1], F32, kind="ExternalInput").ap()
    ident_in = nc.dram_tensor("ident_in", [128, 128], F32, kind="ExternalInput").ap()
    maskDT_in = nc.dram_tensor("maskDT_in", [128, 128], BF16,
                               kind="ExternalInput").ap()
    maskUT_in = nc.dram_tensor("maskUT_in", [128, 128], BF16,
                               kind="ExternalInput").ap()
    cosk_in = nc.dram_tensor("cosk_in", [128, L], F32, kind="ExternalInput").ap()
    sink_in = nc.dram_tensor("sink_in", [128, L], F32, kind="ExternalInput").ap()
    bsel_in = nc.dram_tensor("bsel_in", [BPC, BPC * 128], F32,
                             kind="ExternalInput").ap()
    y_out = nc.dram_tensor("y_out", [BPC, L, D], F32, kind="ExternalOutput").ap()

    qh_sp = nc.dram_tensor("qh_sp", [BPC, KCH, 128, L], BF16).ap()
    ql_sp = nc.dram_tensor("ql_sp", [BPC, KCH, 128, L], BF16).ap()
    kh_sp = nc.dram_tensor("kh_sp", [BPC, KCH, 128, L], BF16).ap()
    kl_sp = nc.dram_tensor("kl_sp", [BPC, KCH, 128, L], BF16).ap()
    v_sph = nc.dram_tensor("v_sph", [BPC, NQT, 128, H * 65], BF16).ap()
    v_spl = nc.dram_tensor("v_spl", [BPC, NQT, 128, H * 65], BF16).ap()
    x1_sp = nc.dram_tensor("x1_sp", [BPC, NQT, 128, D], F32).ap()
    h2_sp = nc.dram_tensor("h2_sp", [BPC, NQT, 128, D], F32).ap()
    hm_sp = nc.dram_tensor("hm_sp", [BPC, NBI, 128, L], F32).ap()
    wgq_sp = nc.dram_tensor("wgq_sp", [NBI, 128, KCH * 128], BF16).ap()
    wuq_sp = nc.dram_tensor("wuq_sp", [NBI, 128, KCH * 128], BF16).ap()
    wdq_sp = nc.dram_tensor("wdq_sp", [KCH, 128, NBI * 128], BF16).ap()
    cc1_i = nc.dram_tensor("cc1_i", [1, 8], F32).ap()
    cc1_o = nc.dram_tensor("cc1_o", [1, 8], F32).ap()
    cc2_i = nc.dram_tensor("cc2_i", [1, 8], F32).ap()
    cc2_o = nc.dram_tensor("cc2_o", [1, 8], F32).ap()

    def A(n, s, d=F32):
        return nc.alloc_sbuf_tensor(n, s, d).ap()

    ident = A("ident", [128, 128])
    identb = A("identb", [128, 128], BF16)
    maskDT = A("maskDT", [128, 128], BF16)
    maskUT = A("maskUT", [128, 128], BF16)
    ones1 = A("ones1", [1, 128])
    magicv = A("magicv", [128, 1])
    epsv = A("epsv", [128, 1])
    neg20v = A("neg20v", [128, 1])
    camod_fm = A("camod_fm", [128, 2 * KCH])
    sa_fm = A("sa_fm", [128, 2 * KCH])
    ga_tab = A("ga_tab", [128, 2 * KCH])
    gb_tab = A("gb_tab", [128, 2 * KCH])
    cb_row = A("cb_row", [BPC, D])
    sb_row = A("sb_row", [BPC, D])
    biasq = A("biasq", [128, 2 * NBQ])
    biasq_q = A("biasq_q", [128, 16])
    swg_tab = A("swg_tab", [128, NBI])
    swu_tab = A("swu_tab", [128, NBI])
    swd_tab = A("swd_tab", [128, KCH])
    rwg_tab = A("rwg_tab", [128, NBI])
    rwu_tab = A("rwu_tab", [128, NBI])
    rwd_tab = A("rwd_tab", [128, KCH])
    sxg_tab = A("sxg_tab", [128, NBI])
    sxu_tab = A("sxu_tab", [128, NBI])
    sdgb_tab = A("sdgb_tab", [128, 2 * KCH])
    maxc_a = A("maxc_a", [128, BPC * NQT])
    maxc_b = A("maxc_b", [128, BPC * NBI * NST])
    rxv = A("rxv", [128, 1])
    rmv = A("rmv", [128, 1])
    sm = {n: A("sm_" + n, [128, 1]) for n in
          ("nm0", "nm1", "sum0", "sum1", "rs0", "rs1",
           "ssq", "sumc", "mc", "msqc", "m2c", "varc", "svc", "rstdc", "nmrc",
           "mcol", "rw")}
    s1 = {n: A("s1_" + n, [1, 1]) for n in
          ("m1l", "m1g", "m2l", "m2g", "sigx", "sigd", "sxm", "sxs", "rx1",
           "smm", "sms", "rm1")}
    sc128 = A("sc128", [1, 128])
    z8 = A("z8", [1, 8])
    onesel = A("onesel", [BPC, BPC * 128])
    clipx_sb = A("clipx_sb", [1, 1])
    clipdn_sb = A("clipdn_sb", [1, 1])
    tmp8 = A("tmp8", [128, KCH])

    back = nc.alloc_sbuf_tensor("arena", [128, ARENA_BYTES], U8)
    ARENA_BASE = nc.lookup_mloc(back).addr

    class Ph:
        def __init__(self, tag):
            self.cur = ARENA_BASE
            self.tag = tag

        def t(self, name, shape, dtype=F32):
            per = 1
            for s in shape[1:]:
                per *= s
            per *= mybir.dt.size(dtype)
            off = (self.cur + 31) // 32 * 32
            self.cur = off + per
            assert self.cur <= ARENA_BASE + ARENA_BYTES, \
                (self.tag, name, self.cur - ARENA_BASE)
            return nc.alloc_sbuf_tensor_at(f"{self.tag}_{name}", shape, dtype,
                                           offset=off).ap()

    ps_a = nc.alloc_psum_tensor("ps_a", [128, 640], F32).ap()
    ps_b = nc.alloc_psum_tensor("ps_b", [128, 640], F32).ap()
    ps_c = nc.alloc_psum_tensor("ps_c", [128, 640], F32).ap()
    ps_d = nc.alloc_psum_tensor("ps_d", [128, 512], F32).ap()
    ps_e = nc.alloc_psum_tensor("ps_e", [128, 512], F32).ap()

    def dma(queue, out, in_):
        S.op(queue, lambda e, o=out, i=in_: e.dma_start(out=o, in_=i),
             reads=[in_], writes=[out], kind="d")

    def mm(out, lhsT, rhs, start, stop):
        S.op("pe", lambda e, o=out, l=lhsT, r=rhs, s1_=start, s2_=stop:
             e.matmul(o, l, r, start=s1_, stop=s2_),
             reads=[lhsT, rhs] + ([] if start else [out]), writes=[out])

    def transpose(out_ps, in_sb, idn):
        S.op("pe", lambda e, o=out_ps, i=in_sb, d_=idn: e.transpose(o, i, d_),
             reads=[in_sb, idn], writes=[out_ps])

    def act(out, in_, func, bias=0.0, scale=1.0, accum=None):
        S.op("act", lambda e, o=out, i=in_, f=func, b=bias, s_=scale, a=accum:
             e.activation(o, i, f, bias=b, scale=s_, accum_out=a),
             reads=[in_, bias, scale], writes=[out, accum])

    def ts(out, in_, v1, v2, op1, op2=None):
        if op2 is None:
            S.op("dve", lambda e, o=out, i=in_, a=v1:
                 e.tensor_scalar(o, i, a, None, op1),
                 reads=[in_, v1], writes=[out])
        else:
            S.op("dve", lambda e, o=out, i=in_, a=v1, b=v2:
                 e.tensor_scalar(o, i, a, b, op1, op2),
                 reads=[in_, v1, v2], writes=[out])

    def tt(out, a, b, op, eng="dve"):
        S.op(eng, lambda e, o=out, x=a, y=b: e.tensor_tensor(o, x, y, op),
             reads=[a, b], writes=[out])

    def stt(out, in0, scalar, in1, op0, op1):
        S.op("dve", lambda e, o=out, i0=in0, s_=scalar, i1=in1:
             e.scalar_tensor_tensor(o, i0, s_, i1, op0, op1),
             reads=[in0, scalar, in1], writes=[out])

    def copy(out, in_, eng="dve"):
        if eng == "act":
            S.op(eng, lambda e, o=out, i=in_: e.copy(o, i),
                 reads=[in_], writes=[out])
        else:
            S.op(eng, lambda e, o=out, i=in_: e.tensor_copy(o, i),
                 reads=[in_], writes=[out])

    def recip(out, in_):
        S.op("dve", lambda e, o=out, i=in_: e.reciprocal(o, i),
             reads=[in_], writes=[out])

    def red(out, in_, op, absv=False, neg=False):
        S.op("dve", lambda e, o=out, i=in_, a=absv, n=neg:
             e.tensor_reduce(o, i, AX.X, op, apply_absolute_value=a, negate=n),
             reads=[in_], writes=[out])

    def shuffle_pairs(out, in_):
        mask = []
        for i in range(0, 32, 2):
            mask += [i + 1, i]
        S.op("dve", lambda e, o=out, i=in_, m=mask: e.stream_shuffle(o, i, m),
             reads=[in_], writes=[out])

    def memset(buf, v, eng="dve"):
        S.op(eng, lambda e, o=buf, c=v: e.memset(o, c), reads=[], writes=[buf])

    def quant_chain(src, out_q, scale_ap, tmp1, tmp2, ti32, ceng="dve"):
        act(tmp1, src, AF.Identity, bias=magicv, scale=scale_ap)
        ts(tmp2, tmp1, MAGIC, -QMAX - 1.0, AL.subtract, AL.max)
        ts(tmp1, tmp2, QMAX, None, AL.min)
        copy(ti32, tmp1, eng=ceng)
        copy(out_q, ti32, eng=ceng)

    def ln_stats(x_tile, out_ln, n_elem, sq_scratch):
        act(sq_scratch, x_tile, AF.Square, accum=sm["ssq"])
        red(sm["sumc"], x_tile, AL.add)
        ts(sm["mc"], sm["sumc"], 1.0 / n_elem, None, AL.mult)
        ts(sm["msqc"], sm["ssq"], 1.0 / n_elem, None, AL.mult)
        tt(sm["m2c"], sm["mc"], sm["mc"], AL.mult)
        tt(sm["varc"], sm["msqc"], sm["m2c"], AL.subtract)
        act(sm["svc"], sm["varc"], AF.Sqrt, bias=epsv)
        recip(sm["rstdc"], sm["svc"])
        tt(sm["nmrc"], sm["mc"], sm["rstdc"], AL.mult)
        ts(sm["nmrc"], sm["nmrc"], -1.0, None, AL.mult)
        act(out_ln, x_tile, AF.Identity, bias=sm["nmrc"], scale=sm["rstdc"])

    def collective(cc_in_ap, cc_out_ap):
        S.op("pool", lambda e, i=cc_in_ap, o=cc_out_ap: e.collective_compute(
            "AllReduce", AL.max, replica_groups=[list(range(NC))],
            ins=[i.opt()], outs=[o.opt()]),
            reads=[cc_in_ap], writes=[cc_out_ap], kind="x")

    dma("sync", ident, ident_in)
    copy(identb, ident)
    dma("sync", maskDT, maskDT_in)
    dma("sync", maskUT, maskUT_in)
    memset(ones1, 1.0)
    memset(magicv, MAGIC)
    memset(epsv, 1e-5)
    memset(neg20v, -20.0)
    memset(z8, 0.0)
    dma("sync", onesel, bsel_in)
    dma("sync", clipx_sb, clip_x[0:1])
    dma("sync", clipdn_sb, clip_dn[0:1])
    act(s1["sigx"], clipx_sb, AF.Sigmoid)
    act(s1["sigd"], clipdn_sb, AF.Sigmoid)

    p0 = Ph("p0")
    wmodb = [p0.t("wmodb0", [128, 6 * D])] * 2
    c_sb = p0.t("c_sb", [BPC, D])
    siluc = p0.t("siluc", [BPC, D])
    siluc_fm = p0.t("siluc_fm", [128, 2 * KCH])
    mod_sb = p0.t("mod_sb", [BPC, 6 * D])
    vr1 = p0.t("vr1", [BPC, D])
    vr2 = p0.t("vr2", [BPC, D])
    vr3 = p0.t("vr3", [BPC, D])
    ga_row = p0.t("ga_row", [BPC, D])
    gb_row = p0.t("gb_row", [BPC, D])

    dma("sync", c_sb, c_in)
    act(siluc, c_sb, AF.Silu)
    for k in range(KCH):
        transpose(ps_e[0:128, 0:2], siluc[:, k * 128:(k + 1) * 128], ident[0:2, 0:2])
        copy(siluc_fm[:, 2 * k:2 * k + 2], ps_e[0:128, 0:2], eng="act")
    for k in range(KCH):
        wb = wmodb[k % 2]
        dma("sync", wb, wmod[k * 128:(k + 1) * 128, :])
        for j in range(12):
            mm(ps_a[0:BPC, 0:512], siluc_fm[:, 2 * k:2 * k + 2],
               wb[:, j * 512:(j + 1) * 512], start=True, stop=True)
            if k == 0:
                copy(mod_sb[:, j * 512:(j + 1) * 512], ps_a[0:BPC, 0:512],
                     eng="act")
            else:
                tt(mod_sb[:, j * 512:(j + 1) * 512], ps_a[0:BPC, 0:512],
                   mod_sb[:, j * 512:(j + 1) * 512], AL.add)

    def clamp_g(dst, col0):
        ts(vr1, mod_sb[:, col0:col0 + D], 0.0, 2.0, AL.is_ge, AL.mult)
        ts(vr2, vr1, -1.0, None, AL.add)
        act(vr3, mod_sb[:, col0:col0 + D], AF.Abs)
        ts(vr3, vr3, 0.25, None, AL.max)
        tt(dst, vr2, vr3, AL.mult)

    clamp_g(ga_row, 2 * D)
    clamp_g(gb_row, 5 * D)
    ts(cb_row, mod_sb[:, 4 * D:5 * D], 1.0, None, AL.add)
    copy(sb_row, mod_sb[:, 3 * D:4 * D])
    ts(vr1, mod_sb[:, 1 * D:2 * D], 1.0, None, AL.add)

    def fm_pack(dst_tab, src_rows):
        for k in range(KCH):
            transpose(ps_e[0:128, 0:2], src_rows[:, k * 128:(k + 1) * 128],
                      ident[0:2, 0:2])
            copy(dst_tab[:, 2 * k:2 * k + 2], ps_e[0:128, 0:2], eng="act")

    fm_pack(camod_fm, vr1)
    fm_pack(sa_fm, mod_sb[:, 0:D])
    fm_pack(ga_tab, ga_row)
    fm_pack(gb_tab, gb_row)

    p1 = p0
    wt = [p1.t(f"wt{i}", [128, DI]) for i in range(2)]
    qt1 = [p1.t(f"qt1_{i}", [128, DI]) for i in range(2)]
    qt2 = [p1.t("qt2", [128, DI])] * 2
    qi = [p1.t("qi", [128, DI], I32)] * 2
    qf = [p1.t(f"qf{i}", [128, DI]) for i in range(2)]
    tq = [p1.t(f"tq{i}", [128, NBI * 128], BF16) for i in range(2)]

    def prep_scales(w_ap, n_tiles, tab, tile_cols):
        for it in range(n_tiles):
            b_ = wt[it % 2]
            dma("sync", b_[:, 0:tile_cols], w_ap[it * 128:(it + 1) * 128, :])
            red(tab[:, it:it + 1], b_[:, 0:tile_cols], AL.max, absv=True)

    prep_scales(w_gate, NBI, swg_tab, D)
    prep_scales(w_up, NBI, swu_tab, D)
    prep_scales(w_down, KCH, swd_tab, DI)
    for tab, rwt, n_ in ((swg_tab, rwg_tab, NBI), (swu_tab, rwu_tab, NBI),
                         (swd_tab, rwd_tab, KCH)):
        ts(tab[:, 0:n_], tab[:, 0:n_], 1.0 / QMAX, None, AL.mult)
        recip(rwt[:, 0:n_], tab[:, 0:n_])

    def prep_weight(w_ap, n_tiles, n_blocks, rwt, sp, tile_cols):
        for it in range(n_tiles):
            p_ = it % 2
            b_ = wt[p_]
            dma("sync", b_[:, 0:tile_cols], w_ap[it * 128:(it + 1) * 128, :])
            quant_chain(b_[:, 0:tile_cols], qf[p_][:, 0:tile_cols],
                        rwt[:, it:it + 1], qt1[p_][:, 0:tile_cols],
                        qt2[p_][:, 0:tile_cols], qi[p_][:, 0:tile_cols])
            tb = tq[p_]
            for c in range(n_blocks):
                pst = (ps_c if c % 2 == 0 else ps_d)[:, 0:128]
                transpose(pst, qf[p_][:, c * 128:(c + 1) * 128], ident)
                copy(tb[:, c * 128:(c + 1) * 128], pst, eng="act")
            dma("pool", sp[it], tb[:, 0:n_blocks * 128])

    prep_weight(w_gate, NBI, KCH, rwg_tab, wgq_sp, D)
    prep_weight(w_up, NBI, KCH, rwu_tab, wuq_sp, D)
    prep_weight(w_down, KCH, NBI, rwd_tab, wdq_sp, DI)
    S.barrier()

    p2 = Ph("p2")
    wtmp2b = [p2.t(f"wtmp2b{i}", [128, 3 * D]) for i in range(2)]
    wqh = p2.t("wqh", [128, KCH * 3 * D], BF16)
    wql = p2.t("wql", [128, KCH * 3 * D], BF16)
    x_t = [p2.t(f"x_t{i}", [128, D]) for i in range(2)]
    ln_t = p2.t("ln_t", [128, D])
    lnh = p2.t("lnh", [128, KCH * ST], BF16)
    lnl = p2.t("lnl", [128, KCH * ST], BF16)
    sqs = p2.t("sqs", [128, D])
    vfm = p2.t("vfm", [128, KCH * ST])
    vtmb = p2.t("vtmb", [128, H * 65])
    vth = p2.t("vth", [128, H * 65], BF16)
    vtl = p2.t("vtl", [128, H * 65], BF16)
    qkf = [p2.t(f"qkf{i}", [128, ST]) for i in range(2)]
    rqs = p2.t("rqs", [128, ST])
    rt1 = p2.t("rt1", [128, ST])
    rhib = [p2.t(f"rhib{i}", [128, ST], BF16) for i in range(2)]
    rlob = [p2.t(f"rlob{i}", [128, ST], BF16) for i in range(2)]
    tabc = p2.t("tabc", [128, ST])
    tabs_ = p2.t("tabs", [128, ST])

    memset(vtmb.rearrange("p (h c) -> p h c", c=65)[:, :, 64:65], 1.0)

    for b in range(BPC):
        for k in range(KCH):
            wtmp = wtmp2b[k % 2]
            dma("sync", wtmp, w_qkv[k * 128:(k + 1) * 128, :])
            if b == 0:
                for nb in range(NBQ):
                    mm(ps_e[:, 2 * nb:2 * nb + 2],
                       wtmp[:, nb * 128:(nb + 1) * 128], sa_fm[:, 2 * k:2 * k + 2],
                       start=(k == 0 and nb == 0),
                       stop=(k == KCH - 1 and nb == NBQ - 1))
            ts(wtmp, wtmp, camod_fm[:, 2 * k + b:2 * k + b + 1], None, AL.mult)
            copy(wqh[:, k * 3 * D:(k + 1) * 3 * D], wtmp)
            stt(wql[:, k * 3 * D:(k + 1) * 3 * D], wtmp, 1.0,
                wqh[:, k * 3 * D:(k + 1) * 3 * D], AL.mult, AL.subtract)
        if b == 0:
            copy(biasq, ps_e[:, 0:2 * NBQ], eng="act")
            ts(biasq_q, biasq[:, 0:16], 0.125, None, AL.mult)

        for st in range(NST):
            t0 = st * ST
            dma("sync", tabc, cosk_in[:, t0:t0 + ST])
            dma("sync", tabs_, sink_in[:, t0:t0 + ST])
            for t in range(4):
                xt_ = x_t[t % 2]
                dma("sync", xt_, x_in[b, t0 + t * 128:t0 + (t + 1) * 128, :])
                ln_stats(xt_, ln_t, D, sqs)
                for c in range(KCH):
                    pst = (ps_c if c % 2 == 0 else ps_d)[:, 0:128]
                    transpose(pst, ln_t[:, c * 128:(c + 1) * 128], ident)
                    act(lnh[:, c * ST + t * 128:c * ST + (t + 1) * 128], pst,
                        AF.Identity)
                    stt(lnl[:, c * ST + t * 128:c * ST + (t + 1) * 128], pst, 1.0,
                        lnh[:, c * ST + t * 128:c * ST + (t + 1) * 128],
                        AL.mult, AL.subtract)
            for nb in range(NBQ):
                ps = (ps_a if nb % 2 == 0 else ps_b)[:, 0:ST]
                passes = [(wqh, lnh), (wqh, lnl), (wql, lnh)]
                nmm = len(passes) * KCH
                i = 0
                for (wsrc, xsrc) in passes:
                    for k in range(KCH):
                        mm(ps, wsrc[:, k * 3 * D + nb * 128:k * 3 * D + (nb + 1) * 128],
                           xsrc[:, k * ST:(k + 1) * ST],
                           start=(i == 0), stop=(i == nmm - 1))
                        i += 1
                if nb >= 16:
                    c = nb - 16
                    act(vfm[:, c * ST:(c + 1) * ST], ps, AF.Identity,
                        bias=biasq[:, 2 * nb + b:2 * nb + b + 1])
                else:
                    qk = qkf[nb % 2]
                    if nb < 8:
                        act(qk, ps, AF.Identity, scale=0.125,
                            bias=biasq_q[:, 2 * nb + b:2 * nb + b + 1])
                    else:
                        act(qk, ps, AF.Identity,
                            bias=biasq[:, 2 * nb + b:2 * nb + b + 1])
                    shuffle_pairs(rqs, qk)
                    tt(rt1, qk, tabc, AL.mult)
                    tt(rqs, rqs, tabs_, AL.mult)
                    tt(rt1, rt1, rqs, AL.add)
                    hib, lob = rhib[nb % 2], rlob[nb % 2]
                    copy(hib, rt1)
                    stt(lob, rt1, 1.0, hib, AL.mult, AL.subtract)
                    c = nb if nb < 8 else nb - 8
                    sph, spl = (qh_sp, ql_sp) if nb < 8 else (kh_sp, kl_sp)
                    dma("pool", sph[b, c, :, t0:t0 + ST], hib)
                    dma("pool", spl[b, c, :, t0:t0 + ST], lob)
            for t in range(4):
                for c in range(KCH):
                    pst = (ps_c if c % 2 == 0 else ps_d)[:, 0:128]
                    transpose(pst, vfm[:, c * ST + t * 128:c * ST + (t + 1) * 128],
                              ident)
                    copy(vtmb[:, (2 * c) * 65:(2 * c) * 65 + 64], pst[:, 0:64],
                         eng="act")
                    copy(vtmb[:, (2 * c + 1) * 65:(2 * c + 1) * 65 + 64],
                         pst[:, 64:128], eng="act")
                copy(vth, vtmb)
                stt(vtl, vtmb, 1.0, vth, AL.mult, AL.subtract)
                dma("pool", v_sph[b, st * 4 + t], vth)
                dma("pool", v_spl[b, st * 4 + t], vtl)
    S.barrier()

    p3 = Ph("p3")
    wtmp2 = p3.t("wtmp2", [128, D])
    wouth = p3.t("wouth", [128, KCH * D], BF16)
    woutl = p3.t("woutl", [128, KCH * D], BF16)
    khr = [p3.t(f"khr{i}", [128, KCH * 3 * 128], BF16) for i in range(2)]
    klr = [p3.t(f"klr{i}", [128, KCH * 3 * 128], BF16) for i in range(2)]
    vtrh = [p3.t(f"vtrh{i}", [128, 3 * H * 65], BF16) for i in range(2)]
    vtrl = [p3.t(f"vtrl{i}", [128, 3 * H * 65], BF16) for i in range(2)]
    qbh = [p3.t(f"qbh{i}", [128, KCH * 128], BF16) for i in range(2)]
    qbl = [p3.t(f"qbl{i}", [128, KCH * 128], BF16) for i in range(2)]
    x_t2 = [p3.t(f"x_t2{i}", [128, D]) for i in range(2)]
    prf = [p3.t(f"prf{i}", [128, 640]) for i in range(3)]
    prth = [p3.t(f"prth{i}", [128, 640], BF16) for i in range(3)]
    prtl = [p3.t(f"prtl{i}", [128, 640], BF16) for i in range(3)]
    attn_tm = [p3.t(f"attn_tm{i}", [128, D]) for i in range(4)]
    attn_fmh = p3.t("attn_fmh", [128, KCH * ST], BF16)
    attn_fml = p3.t("attn_fml", [128, KCH * ST], BF16)
    projga = [p3.t(f"projga{i}", [128, ST]) for i in range(2)]
    x1_t = [p3.t(f"x1_t{i}", [128, D]) for i in range(4)]
    ln2b = p3.t("ln2b", [128, D])
    h2t = [p3.t(f"h2t{i}", [128, D]) for i in range(2)]
    sqs2 = p3.t("sqs2", [128, D])
    cbrep = p3.t("cbrep", [128, D])
    sbrep = p3.t("sbrep", [128, D])

    for k in range(KCH):
        dma("sync", wtmp2, w_out[k * 128:(k + 1) * 128, :])
        copy(wouth[:, k * D:(k + 1) * D], wtmp2)
        stt(woutl[:, k * D:(k + 1) * D], wtmp2, 1.0,
            wouth[:, k * D:(k + 1) * D], AL.mult, AL.subtract)

    def ring_spans(kbs):
        spans = []
        j = 0
        n = len(kbs)
        while j < n:
            s0 = kbs[j] % RING
            ln_ = 1
            while (j + ln_ < n and (kbs[j + ln_] % RING) == s0 + ln_
                   and ln_ < 4 and (j + ln_) != 4):
                ln_ += 1
            spans.append((j, ln_, s0))
            j += ln_
        return spans

    for b in range(BPC):
        for half in range(2):
            mm(ps_d[0:128, 0:512], onesel[:, b * 128:(b + 1) * 128],
               cb_row[:, half * 512:(half + 1) * 512], start=True, stop=True)
            copy(cbrep[:, half * 512:(half + 1) * 512], ps_d[0:128, 0:512], eng="act")
            mm(ps_e[0:128, 0:512], onesel[:, b * 128:(b + 1) * 128],
               sb_row[:, half * 512:(half + 1) * 512], start=True, stop=True)
            copy(sbrep[:, half * 512:(half + 1) * 512], ps_e[0:128, 0:512], eng="act")
        def a2_loads(qt):
            pq = qt % 2
            slot = qt % RING
            for c_ in range(KCH):
                tsl = slice(qt * 128, (qt + 1) * 128)
                dma("sync", qbh[pq][:, c_ * 128:(c_ + 1) * 128],
                    qh_sp[b, c_, :, tsl])
                dma("sync", qbl[pq][:, c_ * 128:(c_ + 1) * 128],
                    ql_sp[b, c_, :, tsl])
                ro = c_ * 3 * 128 + (slot // 2) * 128
                dma("sync", khr[slot % 2][:, ro:ro + 128], kh_sp[b, c_, :, tsl])
                dma("sync", klr[slot % 2][:, ro:ro + 128], kl_sp[b, c_, :, tsl])
            vo0 = (slot // 2) * H * 65
            dma("sync", vtrh[slot % 2][:, vo0:vo0 + H * 65], v_sph[b, qt])
            dma("sync", vtrl[slot % 2][:, vo0:vo0 + H * 65], v_spl[b, qt])

        if _LIMIT_QT is None:
            a2_loads(0)
        for st in range(NST if _LIMIT_QT is None else 0):
            for tq in range(4):
                qt = st * 4 + tq
                pq = qt % 2
                slot = qt % RING
                if qt + 1 < NQT:
                    a2_loads(qt + 1)

                nk = min(qt, 4) + 1
                kbs = list(range(qt - nk + 1, qt + 1))
                sps = [ps_a, ps_b, ps_c]

                def scores_T(h):
                    c = h // 2
                    r0 = 64 * (h % 2)
                    ps = sps[h % 3]
                    passes = [(khr, qbh[pq]), (khr, qbl[pq]), (klr, qbh[pq])]
                    ops = []
                    for ip, (ksrc, qsrc) in enumerate(passes):
                        for j, kb_ in enumerate(kbs):
                            ops.append(("s", ip, j, kb_ % RING))
                    ops.append(("mD", nk - 1))
                    if nk == 5:
                        ops.append(("mU", 0))
                    reg = lambda j: 0 if j < 4 else 1
                    first_r, last_r = {}, {}
                    for i_, o_ in enumerate(ops):
                        r = reg(o_[2] if o_[0] == "s" else o_[1])
                        if r not in first_r:
                            first_r[r] = i_
                        last_r[r] = i_
                    out = []
                    for i_, o_ in enumerate(ops):
                        if o_[0] == "s":
                            _, ip, j, s0 = o_
                            ksrc, qsrc = passes[ip]
                            kt = ksrc[s0 % 2]
                            ko = c * 3 * 128 + (s0 // 2) * 128
                            out.append((ps[:, j * 128:(j + 1) * 128],
                                        kt[r0:r0 + 64, ko:ko + 128],
                                        qsrc[r0:r0 + 64, c * 128:(c + 1) * 128],
                                        first_r[reg(j)] == i_,
                                        last_r[reg(j)] == i_))
                        else:
                            j = o_[1]
                            mtile = maskDT if o_[0] == "mD" else maskUT
                            out.append((ps[:, j * 128:(j + 1) * 128],
                                        identb, mtile,
                                        first_r[reg(j)] == i_,
                                        last_r[reg(j)] == i_))
                    return out

                def tail(h):
                    p3_ = h % 3
                    ps = sps[p3_]
                    act(prth[p3_][:, 0:nk * 128], ps[:, 0:nk * 128], AF.Exp,
                        bias=neg20v)
                    act(prf[p3_][:, 0:nk * 128], ps[:, 0:nk * 128], AF.Exp,
                        bias=neg20v)
                    stt(prtl[p3_][:, 0:nk * 128], prf[p3_][:, 0:nk * 128], 1.0,
                        prth[p3_][:, 0:nk * 128], AL.mult, AL.subtract)
                    pv = (ps_d if h % 2 == 0 else ps_e)[:, 0:65]
                    pvp = [(prth[p3_], vtrh), (prth[p3_], vtrl), (prtl[p3_], vtrh)]
                    nmm2 = 3 * nk
                    i2_ = 0
                    for (psrc, vsrc) in pvp:
                        for j, kb_ in enumerate(kbs):
                            sl_ = kb_ % RING
                            vo = (sl_ // 2) * H * 65 + h * 65
                            mm(pv, psrc[:, j * 128:(j + 1) * 128],
                               vsrc[sl_ % 2][:, vo:vo + 65],
                               start=(i2_ == 0), stop=(i2_ == nmm2 - 1))
                            i2_ += 1
                    rsv = sm["rs0" if h % 2 == 0 else "rs1"]
                    recip(rsv, pv[:, 64:65])
                    act(attn_tm[tq][:, h * 64:h * 64 + 64], pv[:, 0:64],
                        AF.Identity, scale=rsv)

                for hp in range(8):
                    se = scores_T(2 * hp)
                    so = scores_T(2 * hp + 1)
                    ni = max(len(se), len(so))
                    for i_ in range(ni):
                        if i_ < len(se):
                            mm(*se[i_])
                        if i_ < len(so):
                            mm(*so[i_])
                    if hp > 0:
                        tail(2 * hp - 1)
                    tail(2 * hp)
                tail(15)
            for tq in range(4):
                for c in range(KCH):
                    pst = (ps_c if c % 2 == 0 else ps_d)[:, 0:128]
                    transpose(pst, attn_tm[tq][:, c * 128:(c + 1) * 128], ident)
                    col = c * ST + tq * 128
                    act(attn_fmh[:, col:col + 128], pst, AF.Identity)
                    stt(attn_fml[:, col:col + 128], pst, 1.0,
                        attn_fmh[:, col:col + 128], AL.mult, AL.subtract)
            for n in range(KCH):
                ps = (ps_a if n % 2 == 0 else ps_b)[:, 0:ST]
                passes2 = [(wouth, attn_fmh), (wouth, attn_fml), (woutl, attn_fmh)]
                nmm = 3 * KCH
                i_ = 0
                for (wsrc, xsrc) in passes2:
                    for k in range(KCH):
                        mm(ps, wsrc[:, k * D + n * 128:k * D + (n + 1) * 128],
                           xsrc[:, k * ST:(k + 1) * ST],
                           start=(i_ == 0), stop=(i_ == nmm - 1))
                        i_ += 1
                act(projga[n % 2], ps, AF.Identity,
                    scale=ga_tab[:, 2 * n + b:2 * n + b + 1])
                for t in range(4):
                    pst = (ps_c if t % 2 == 0 else ps_d)[:, 0:128]
                    transpose(pst, projga[n % 2][:, t * 128:(t + 1) * 128], ident)
                    copy(x1_t[t][:, n * 128:(n + 1) * 128], pst)
            for t in range(4):
                qt = st * 4 + t
                dma("sync", x_t2[t % 2], x_in[b, qt * 128:(qt + 1) * 128, :])
                tt(x1_t[t], x1_t[t], x_t2[t % 2], AL.add, eng="pool")
                dma("pool", x1_sp[b, qt], x1_t[t])
                ln_stats(x1_t[t], ln2b, D, sqs2)
                tt(h2t[t % 2], ln2b, cbrep, AL.mult, eng="pool")
                tt(h2t[t % 2], h2t[t % 2], sbrep, AL.add, eng="pool")
                red(maxc_a[:, b * NQT + qt:b * NQT + qt + 1], h2t[t % 2],
                    AL.max, absv=True)
                dma("pool", h2_sp[b, qt], h2t[t % 2])
    red(sm["mcol"], maxc_a, AL.max)
    transpose(ps_e[0:1, 0:128], sm["mcol"], ident)
    copy(sc128, ps_e[0:1, 0:128], eng="act")
    red(s1["m1l"], sc128, AL.max)
    copy(z8[:, 0:1], s1["m1l"])
    dma("pool", cc1_i, z8)
    collective(cc1_i, cc1_o)
    dma("pool", s1["m1g"], cc1_o[0:1, 0:1])
    tt(s1["sxm"], s1["sigx"], s1["m1g"], AL.mult)
    ts(s1["sxs"], s1["sxm"], 1.0 / QMAX, None, AL.mult)
    recip(s1["rx1"], s1["sxs"])
    mm(ps_e[0:128, 0:1], ones1, s1["rx1"], start=True, stop=True)
    copy(rxv, ps_e[0:128, 0:1], eng="act")
    mm(ps_e[0:128, 1:2], ones1, s1["sxs"], start=True, stop=True)
    copy(sm["rw"], ps_e[0:128, 1:2], eng="act")
    ts(sxg_tab, swg_tab, sm["rw"], None, AL.mult)
    ts(sxu_tab, swu_tab, sm["rw"], None, AL.mult)
    S.barrier()

    p4 = Ph("p4")
    wgq_sb = p4.t("wgq_sb", [128, NBI * KCH * 128], BF16)
    wuq_sb = p4.t("wuq_sb", [128, NBI * KCH * 128], BF16)
    h2b = [p4.t(f"h2b{i}", [128, D]) for i in range(2)]
    qb1 = [p4.t(f"qb1_{i}", [128, D]) for i in range(2)]
    qb2 = [p4.t(f"qb2_{i}", [128, D]) for i in range(2)]
    qbi = [p4.t(f"qbi{i}", [128, D], I32) for i in range(2)]
    qbf = [p4.t(f"qbf{i}", [128, D]) for i in range(2)]
    h2q = [p4.t(f"h2q{i}", [128, KCH * ST], BF16) for i in range(2)]
    silu_b = [p4.t(f"silu_b{i}", [128, ST]) for i in range(2)]
    hmid = [p4.t(f"hmid{i}", [128, ST]) for i in range(2)]

    for i_ in range(NBI):
        dma("sync", wgq_sb[:, i_ * KCH * 128:(i_ + 1) * KCH * 128], wgq_sp[i_])
        dma("sync", wuq_sb[:, i_ * KCH * 128:(i_ + 1) * KCH * 128], wuq_sp[i_])

    def b1_quant(b, st):
        sp_ = (b * NST + st) % 2
        for t in range(4):
            hb_ = h2b[t % 2]
            dma("sync", hb_, h2_sp[b, st * 4 + t])
            quant_chain(hb_, qbf[t % 2], rxv, qb1[t % 2], qb2[t % 2],
                        qbi[t % 2])
            for c in range(KCH):
                pst = (ps_a if c % 2 == 0 else ps_b)[:, 512:640]
                transpose(pst, qbf[t % 2][:, c * 128:(c + 1) * 128], ident)
                copy(h2q[sp_][:, c * ST + t * 128:c * ST + (t + 1) * 128],
                     pst, eng="act")

    def b1_mms(b, st):
        sp_ = (b * NST + st) % 2
        for ib in range(NBI):
            pg = (ps_a if ib % 2 == 0 else ps_b)[:, 0:ST]
            for k in range(KCH):
                mm(pg, wgq_sb[:, ib * D + k * 128:ib * D + (k + 1) * 128],
                   h2q[sp_][:, k * ST:(k + 1) * ST],
                   start=(k == 0), stop=(k == KCH - 1))
            act(silu_b[ib % 2], pg, AF.Silu, scale=sxg_tab[:, ib:ib + 1])
            pu = (ps_c if ib % 2 == 0 else ps_d)[:, 0:ST]
            for k in range(KCH):
                mm(pu, wuq_sb[:, ib * D + k * 128:ib * D + (k + 1) * 128],
                   h2q[sp_][:, k * ST:(k + 1) * ST],
                   start=(k == 0), stop=(k == KCH - 1))
            stt(hmid[ib % 2], pu, sxu_tab[:, ib:ib + 1], silu_b[ib % 2],
                AL.mult, AL.mult)
            red(maxc_b[:, (b * NST + st) * NBI + ib:
                       (b * NST + st) * NBI + ib + 1],
                hmid[ib % 2], AL.max, absv=True)
            dma("pool", hm_sp[b, ib, :, st * ST:(st + 1) * ST], hmid[ib % 2])

    prev = None
    for b in range(BPC):
        for st in range(NST):
            b1_quant(b, st)
            if prev is not None:
                b1_mms(*prev)
            prev = (b, st)
    b1_mms(*prev)
    red(sm["mcol"], maxc_b, AL.max)
    transpose(ps_e[0:1, 0:128], sm["mcol"], ident)
    copy(sc128, ps_e[0:1, 0:128], eng="act")
    red(s1["m2l"], sc128, AL.max)
    copy(z8[:, 0:1], s1["m2l"])
    dma("pool", cc2_i, z8)
    collective(cc2_i, cc2_o)
    dma("pool", s1["m2g"], cc2_o[0:1, 0:1])
    tt(s1["smm"], s1["sigd"], s1["m2g"], AL.mult)
    ts(s1["sms"], s1["smm"], 1.0 / QMAX, None, AL.mult)
    recip(s1["rm1"], s1["sms"])
    mm(ps_e[0:128, 0:1], ones1, s1["rm1"], start=True, stop=True)
    copy(rmv, ps_e[0:128, 0:1], eng="act")
    mm(ps_e[0:128, 1:2], ones1, s1["sms"], start=True, stop=True)
    copy(sm["rw"], ps_e[0:128, 1:2], eng="act")
    ts(tmp8, swd_tab, sm["rw"], None, AL.mult)
    for b in range(BPC):
        for hc in range(KCH):
            tt(sdgb_tab[:, 2 * hc + b:2 * hc + b + 1], tmp8[:, hc:hc + 1],
               gb_tab[:, 2 * hc + b:2 * hc + b + 1], AL.mult)
    S.barrier()

    p5 = Ph("p5")
    wdq_sb = p5.t("wdq_sb", [128, KCH * NBI * 128], BF16)
    hmin = [p5.t(f"hmin{i}", [128, ST]) for i in range(2)]
    qc1 = [p5.t(f"qc1_{i}", [128, ST]) for i in range(2)]
    qc2 = [p5.t(f"qc2_{i}", [128, ST]) for i in range(2)]
    qci = [p5.t(f"qci{i}", [128, ST], I32) for i in range(2)]
    hmq = [p5.t(f"hmq{i}", [128, NBI * ST], BF16) for i in range(2)]
    x1b = [p5.t(f"x1b{i}", [128, D]) for i in range(4)]
    dwn = [p5.t(f"dwn{i}", [128, ST]) for i in range(2)]
    xo = [p5.t(f"xo{i}", [128, D]) for i in range(4)]

    for c_ in range(KCH):
        dma("sync", wdq_sb[:, c_ * NBI * 128:(c_ + 1) * NBI * 128], wdq_sp[c_])
    def b2_quant(b, st):
        sp_ = (b * NST + st) % 2
        for ic in range(NBI):
            hm_ = hmin[ic % 2]
            dma("sync", hm_, hm_sp[b, ic, :, st * ST:(st + 1) * ST])
            quant_chain(hm_, hmq[sp_][:, ic * ST:(ic + 1) * ST], rmv,
                        qc1[ic % 2], qc2[ic % 2], qci[ic % 2])

    def b2_mms(b, st):
        sp_ = (b * NST + st) % 2
        for t in range(4):
            dma("sync", x1b[t], x1_sp[b, st * 4 + t])
        for hb in range(KCH):
            ps = (ps_a if hb % 2 == 0 else ps_b)[:, 0:ST]
            for ic in range(NBI):
                mm(ps, wdq_sb[:, hb * DI + ic * 128:hb * DI + (ic + 1) * 128],
                   hmq[sp_][:, ic * ST:(ic + 1) * ST],
                   start=(ic == 0), stop=(ic == NBI - 1))
            act(dwn[hb % 2], ps, AF.Identity,
                scale=sdgb_tab[:, 2 * hb + b:2 * hb + b + 1])
            for t in range(4):
                pst = (ps_c if t % 2 == 0 else ps_d)[:, 0:128]
                transpose(pst, dwn[hb % 2][:, t * 128:(t + 1) * 128], ident)
                tt(xo[t][:, hb * 128:(hb + 1) * 128], pst,
                   x1b[t][:, hb * 128:(hb + 1) * 128], AL.add)
        for t in range(4):
            dma("pool", y_out[b, st * ST + t * 128:st * ST + (t + 1) * 128, :],
                xo[t])

    prev = None
    for b in range(BPC):
        for st in range(NST):
            b2_quant(b, st)
            if prev is not None:
                b2_mms(*prev)
            prev = (b, st)
    b2_mms(*prev)
    S.barrier()

    S.finalize()
    from contextlib import ExitStack
    with ExitStack() as stack:
        sems = {}
        for e in ENGINES:
            for k in ("c", "x"):
                sems[(e, k)] = stack.enter_context(nc.semaphore(f"sem_{e}_{k}"))
            for ln_ in range(S.NLANES):
                sems[(e, "d", ln_)] = stack.enter_context(
                    nc.semaphore(f"sem_{e}_d{ln_}"))
        block = stack.enter_context(nc.Block())

        def run_engine(handle, ename):
            for o in S.q[ename]:
                for (key, v) in o.waits:
                    handle.wait_ge(sems[key], v)
                instr = o.fn(handle)
                if o.inc:
                    if o.kind == "d":
                        instr.then_inc(sems[(ename, "d", o.lane)], 16)
                    else:
                        instr.then_inc(sems[(ename, o.kind)], 1)
            if ename == "pool":
                for ln_ in range(S.NLANES):
                    fv = S.lane_final["pool"][ln_]
                    if fv > 0:
                        handle.wait_ge(sems[("pool", "d", ln_)], fv)

        @block.tensor
        def _(engh):
            run_engine(engh, "pe")

        @block.vector
        def _(engh):
            run_engine(engh, "dve")

        @block.scalar
        def _(engh):
            run_engine(engh, "act")

        @block.gpsimd
        def _(engh):
            run_engine(engh, "pool")

        @block.sync
        def _(engh):
            run_engine(engh, "sync")

    return nc



def _host_constants():
    import ml_dtypes
    ident = np.eye(128, dtype=np.float32)
    kk, qq = np.meshgrid(np.arange(128), np.arange(128), indexing="ij")
    maskDT = np.where(qq >= kk, 0.0, -1e30).astype(ml_dtypes.bfloat16)
    maskUT = np.where(kk > qq, 0.0, -1e30).astype(ml_dtypes.bfloat16)
    inv = (1.0 / (np.float32(10000.0) **
                  (np.arange(0, HD, 2, dtype=np.float32) / np.float32(HD))))
    inv = inv.astype(np.float32)
    tpos = np.arange(L, dtype=np.float32)
    freqs = (inv[:, None] * tpos[None, :]).astype(np.float32)
    cos1 = np.cos(freqs).astype(np.float32)
    sin1 = np.sin(freqs).astype(np.float32)
    cos64 = np.repeat(cos1, 2, axis=0)
    sin64 = np.repeat(sin1, 2, axis=0)
    sgn = np.where(np.arange(HD) % 2 == 0, -1.0, 1.0).astype(np.float32)[:, None]
    sinS = (sin64 * sgn).astype(np.float32)
    cos128 = np.concatenate([cos64, cos64], axis=0).astype(np.float32)
    sinS128 = np.concatenate([sinS, sinS], axis=0).astype(np.float32)
    bsel = np.zeros((BPC, BPC * 128), dtype=np.float32)
    for b_ in range(BPC):
        bsel[b_, b_ * 128:(b_ + 1) * 128] = 1.0
    return dict(ident_in=ident, maskDT_in=maskDT, maskUT_in=maskUT,
                cosk_in=cos128, sink_in=sinS128, bsel_in=bsel)


_NC_CACHE = None


def kernel(L0_attn_x, L0_attn_w_qkv, L0_attn_w_out, L0_ffn_clip_x,
           L0_ffn_clip_down, L0_ffn_w_gate, L0_ffn_w_up, L0_ffn_w_down,
           L0_c, L0_wmod):
    global _NC_CACHE
    if _NC_CACHE is None:
        _NC_CACHE = build_program()
    nc = _NC_CACHE
    consts = _host_constants()
    x = np.ascontiguousarray(np.asarray(L0_attn_x, dtype=np.float32))
    cvec = np.ascontiguousarray(np.asarray(L0_c, dtype=np.float32))
    shared = dict(
        w_qkv=np.ascontiguousarray(np.asarray(L0_attn_w_qkv, np.float32)),
        w_out=np.ascontiguousarray(np.asarray(L0_attn_w_out, np.float32)),
        wmod=np.ascontiguousarray(np.asarray(L0_wmod, np.float32)),
        w_gate=np.ascontiguousarray(np.asarray(L0_ffn_w_gate, np.float32)),
        w_up=np.ascontiguousarray(np.asarray(L0_ffn_w_up, np.float32)),
        w_down=np.ascontiguousarray(np.asarray(L0_ffn_w_down, np.float32)),
        clip_x=np.ascontiguousarray(np.asarray(L0_ffn_clip_x, np.float32)),
        clip_dn=np.ascontiguousarray(np.asarray(L0_ffn_clip_down, np.float32)),
        **consts,
    )
    in_maps = []
    for c in range(NC):
        m = dict(shared)
        m["x_in"] = np.ascontiguousarray(x[BPC * c:BPC * (c + 1)])
        m["c_in"] = np.ascontiguousarray(cvec[BPC * c:BPC * (c + 1)])
        in_maps.append(m)
    res = run_bass_kernel_spmd(nc, in_maps, core_ids=list(range(NC)))
    return np.concatenate([np.asarray(r["y_out"]) for r in res.results], axis=0)


if __name__ == "__main__":
    import time
    t0 = time.time()
    nc = build_program()
    n = sum(1 for _ in nc.all_instructions())
    print(f"built ok in {time.time() - t0:.1f}s, {n} instructions")


# version 15
# speedup vs baseline: 1.0668x; 1.0668x over previous
import numpy as np

import concourse.bass as bass
import concourse.mybir as mybir
from concourse.bass_utils import run_bass_kernel_spmd

F32 = mybir.dt.float32
BF16 = mybir.dt.bfloat16
I32 = mybir.dt.int32
U8 = mybir.dt.uint8
AL = mybir.AluOpType
AF = mybir.ActivationFunctionType
AX = mybir.AxisListType

NC = 8
BPC = 2
L = 2048
D = 1024
H = 16
HD = 64
WIN = 512
DI = 2816
KCH = D // 128
NBQ = 3 * D // 128
NBI = DI // 128
ST = 512
NST = L // ST
NQT = L // 128
RING = 6

MAGIC = 12582912.0
FMIN = float(np.finfo(np.float32).min)
QMAX = 127.0
ARENA_BYTES = 198000

_LIMIT_QT = None


ENGINES = ("pe", "dve", "act", "pool", "sync")


class _Op:
    __slots__ = ("eng", "fn", "deps", "idx", "inc", "kind", "waits", "lane")

    def __init__(self, eng, fn, deps, idx, kind):
        self.eng = eng
        self.fn = fn
        self.deps = deps
        self.idx = idx
        self.inc = False
        self.kind = kind
        self.waits = None
        self.lane = None


def _names(items):
    out = []
    for a in items:
        if a is None or isinstance(a, (int, float)):
            continue
        t = getattr(a, "tensor", None)
        if t is not None:
            nm = t.name
            if type(t).__name__.startswith("PSum"):
                try:
                    dts = mybir.dt.size(a.dtype)
                    off = int(a.offset) * dts
                    span = 1
                    pairs = [tuple(p) for p in a.ap][1:]
                    for st_, ct_ in pairs:
                        span += (int(ct_) - 1) * abs(int(st_))
                    b0 = off // 2048
                    b1 = (off + span * dts - 1) // 2048
                    for bb in range(b0, b1 + 1):
                        out.append(f"{nm}#b{bb}")
                    continue
                except Exception:
                    pass
            out.append(nm)
        elif hasattr(a, "name"):
            out.append(a.name)
    return out


class Sched:
    NLANES = 8

    def __init__(self):
        self.q = {e: [] for e in ENGINES}
        self.last_w = {}
        self.last_r = {}
        self.pending_barrier = {e: [] for e in ENGINES}
        self.dma_count = {e: 0 for e in ENGINES}

    def op(self, eng, fn, reads=(), writes=(), kind="c"):
        idx = len(self.q[eng])
        deps = {}

        def add(p):
            if p is None:
                return
            e2, i2 = p
            if e2 == eng and i2 >= idx:
                return
            if e2 == "pe" and eng == "pe":
                return
            o2 = self.q[e2][i2]
            if o2.kind == "d":
                key = (e2, "d", o2.lane)
            else:
                key = (e2, o2.kind)
            deps[key] = max(deps.get(key, -1), i2)

        rnames = _names(reads)
        wnames = _names(writes)
        for nm in rnames:
            add(self.last_w.get(nm))
        for nm in wnames:
            add(self.last_w.get(nm))
            for e2, i2 in self.last_r.get(nm, {}).items():
                add((e2, i2))
        for p in self.pending_barrier[eng]:
            add(p)
        self.pending_barrier[eng] = []
        o = _Op(eng, fn, list(deps.items()), idx, kind)
        if kind == "d":
            o.lane = self.dma_count[eng] % self.NLANES
            self.dma_count[eng] += 1
        self.q[eng].append(o)
        for nm in rnames:
            self.last_r.setdefault(nm, {})[eng] = idx
        for nm in wnames:
            self.last_w[nm] = (eng, idx)
            self.last_r[nm] = {}
        return o

    def barrier(self):
        marks = [(e, len(self.q[e]) - 1) for e in ENGINES if self.q[e]]
        for e in ENGINES:
            self.pending_barrier[e] = [(e2, i2) for e2, i2 in marks if e2 != e]

    def finalize(self):
        need = {e: set() for e in ENGINES}
        for e in ENGINES:
            for o in self.q[e]:
                for key, i2 in o.deps:
                    need[key[0]].add(i2)
        self.val = {}
        self.lane_final = {e: [0] * self.NLANES for e in ENGINES}
        for e in ENGINES:
            cnt = {"c": 0, "x": 0}
            lane_cnt = [0] * self.NLANES
            for o in self.q[e]:
                if o.kind == "d":
                    o.inc = True
                    lane_cnt[o.lane] += 16
                    self.val[(e, o.idx)] = lane_cnt[o.lane]
                elif o.kind == "x":
                    o.inc = True
                    cnt["x"] += 1
                    self.val[(e, o.idx)] = cnt["x"]
                else:
                    if o.idx in need[e]:
                        o.inc = True
                        cnt["c"] += 1
                    self.val[(e, o.idx)] = cnt["c"]
            self.lane_final[e] = lane_cnt
        nxt = {}
        for e in ENGINES:
            nv = {"c": None, "x": None}
            for o in reversed(self.q[e]):
                if o.inc and o.kind != "d":
                    nv[o.kind] = self.val[(e, o.idx)]
                nxt[(e, o.idx)] = dict(nv)
        for e in ENGINES:
            hw = {}
            for o in self.q[e]:
                fw = []
                if o.kind == "d":
                    v = self.val[(e, o.idx)] - 16
                    if v > 0:
                        key = (e, "d", o.lane)
                        if hw.get(key, 0) < v:
                            hw[key] = v
                            fw.append((key, v))
                for key, i2 in o.deps:
                    if len(key) == 3:
                        v = self.val[(key[0], i2)]
                    else:
                        v = nxt[(key[0], i2)][key[1]]
                        assert v is not None, (e, o.idx, key, i2)
                    if hw.get(key, 0) < v:
                        hw[key] = v
                        fw.append((key, v))
                o.waits = fw
        return self



def build_program():
    nc = bass.Bass("TRN2", target_bir_lowering=False)
    S = Sched()

    x_in = nc.dram_tensor("x_in", [BPC, L, D], F32, kind="ExternalInput").ap()
    c_in = nc.dram_tensor("c_in", [BPC, D], F32, kind="ExternalInput").ap()
    w_qkv = nc.dram_tensor("w_qkv", [D, 3 * D], F32, kind="ExternalInput").ap()
    w_out = nc.dram_tensor("w_out", [D, D], F32, kind="ExternalInput").ap()
    wmod = nc.dram_tensor("wmod", [D, 6 * D], F32, kind="ExternalInput").ap()
    w_gate = nc.dram_tensor("w_gate", [DI, D], F32, kind="ExternalInput").ap()
    w_up = nc.dram_tensor("w_up", [DI, D], F32, kind="ExternalInput").ap()
    w_down = nc.dram_tensor("w_down", [D, DI], F32, kind="ExternalInput").ap()
    clip_x = nc.dram_tensor("clip_x", [1], F32, kind="ExternalInput").ap()
    clip_dn = nc.dram_tensor("clip_dn", [1], F32, kind="ExternalInput").ap()
    ident_in = nc.dram_tensor("ident_in", [128, 128], F32, kind="ExternalInput").ap()
    maskDT_in = nc.dram_tensor("maskDT_in", [128, 128], BF16,
                               kind="ExternalInput").ap()
    maskUT_in = nc.dram_tensor("maskUT_in", [128, 128], BF16,
                               kind="ExternalInput").ap()
    cosk_in = nc.dram_tensor("cosk_in", [128, L], F32, kind="ExternalInput").ap()
    sink_in = nc.dram_tensor("sink_in", [128, L], F32, kind="ExternalInput").ap()
    bsel_in = nc.dram_tensor("bsel_in", [BPC, BPC * 128], F32,
                             kind="ExternalInput").ap()
    y_out = nc.dram_tensor("y_out", [BPC, L, D], F32, kind="ExternalOutput").ap()

    qh_sp = nc.dram_tensor("qh_sp", [BPC, KCH, 128, L], BF16).ap()
    ql_sp = nc.dram_tensor("ql_sp", [BPC, KCH, 128, L], BF16).ap()
    kh_sp = nc.dram_tensor("kh_sp", [BPC, KCH, 128, L], BF16).ap()
    kl_sp = nc.dram_tensor("kl_sp", [BPC, KCH, 128, L], BF16).ap()
    v_sph = nc.dram_tensor("v_sph", [BPC, NQT, 128, H * 65], BF16).ap()
    v_spl = nc.dram_tensor("v_spl", [BPC, NQT, 128, H * 65], BF16).ap()
    x1_sp = nc.dram_tensor("x1_sp", [BPC, NQT, 128, D], F32).ap()
    h2_sp = nc.dram_tensor("h2_sp", [BPC, NQT, 128, D], F32).ap()
    hm_sp = nc.dram_tensor("hm_sp", [BPC, NBI, 128, L], F32).ap()
    wgq_sp = nc.dram_tensor("wgq_sp", [NBI, 128, KCH * 128], BF16).ap()
    wuq_sp = nc.dram_tensor("wuq_sp", [NBI, 128, KCH * 128], BF16).ap()
    wdq_sp = nc.dram_tensor("wdq_sp", [KCH, 128, NBI * 128], BF16).ap()
    cc1_i = nc.dram_tensor("cc1_i", [1, 8], F32).ap()
    cc1_o = nc.dram_tensor("cc1_o", [1, 8], F32).ap()
    cc2_i = nc.dram_tensor("cc2_i", [1, 8], F32).ap()
    cc2_o = nc.dram_tensor("cc2_o", [1, 8], F32).ap()

    def A(n, s, d=F32):
        return nc.alloc_sbuf_tensor(n, s, d).ap()

    ident = A("ident", [128, 128])
    identb = A("identb", [128, 128], BF16)
    maskDT = A("maskDT", [128, 128], BF16)
    maskUT = A("maskUT", [128, 128], BF16)
    ones1 = A("ones1", [1, 128])
    magicv = A("magicv", [128, 1])
    epsv = A("epsv", [128, 1])
    neg20v = A("neg20v", [128, 1])
    camod_fm = A("camod_fm", [128, 2 * KCH])
    sa_fm = A("sa_fm", [128, 2 * KCH])
    ga_tab = A("ga_tab", [128, 2 * KCH])
    gb_tab = A("gb_tab", [128, 2 * KCH])
    cb_row = A("cb_row", [BPC, D])
    sb_row = A("sb_row", [BPC, D])
    biasq = A("biasq", [128, 2 * NBQ])
    biasq_q = A("biasq_q", [128, 16])
    swg_tab = A("swg_tab", [128, NBI])
    swu_tab = A("swu_tab", [128, NBI])
    swd_tab = A("swd_tab", [128, KCH])
    rwg_tab = A("rwg_tab", [128, NBI])
    rwu_tab = A("rwu_tab", [128, NBI])
    rwd_tab = A("rwd_tab", [128, KCH])
    sxg_tab = A("sxg_tab", [128, NBI])
    sxu_tab = A("sxu_tab", [128, NBI])
    sdgb_tab = A("sdgb_tab", [128, 2 * KCH])
    maxc_a = A("maxc_a", [128, BPC * NQT])
    maxc_b = A("maxc_b", [128, BPC * NBI * NST])
    rxv = A("rxv", [128, 1])
    rmv = A("rmv", [128, 1])
    sm = {n: A("sm_" + n, [128, 1]) for n in
          ("nm0", "nm1", "sum0", "sum1", "rs0", "rs1",
           "ssq", "sumc", "mc", "msqc", "m2c", "varc", "svc", "rstdc", "nmrc",
           "mcol", "rw")}
    s1 = {n: A("s1_" + n, [1, 1]) for n in
          ("m1l", "m1g", "m2l", "m2g", "sigx", "sigd", "sxm", "sxs", "rx1",
           "smm", "sms", "rm1")}
    sc128 = A("sc128", [1, 128])
    z8 = A("z8", [1, 8])
    onesel = A("onesel", [BPC, BPC * 128])
    clipx_sb = A("clipx_sb", [1, 1])
    clipdn_sb = A("clipdn_sb", [1, 1])
    tmp8 = A("tmp8", [128, KCH])

    back = nc.alloc_sbuf_tensor("arena", [128, ARENA_BYTES], U8)
    ARENA_BASE = nc.lookup_mloc(back).addr

    class Ph:
        def __init__(self, tag):
            self.cur = ARENA_BASE
            self.tag = tag

        def t(self, name, shape, dtype=F32):
            per = 1
            for s in shape[1:]:
                per *= s
            per *= mybir.dt.size(dtype)
            off = (self.cur + 31) // 32 * 32
            self.cur = off + per
            assert self.cur <= ARENA_BASE + ARENA_BYTES, \
                (self.tag, name, self.cur - ARENA_BASE)
            return nc.alloc_sbuf_tensor_at(f"{self.tag}_{name}", shape, dtype,
                                           offset=off).ap()

    ps_a = nc.alloc_psum_tensor("ps_a", [128, 640], F32).ap()
    ps_b = nc.alloc_psum_tensor("ps_b", [128, 640], F32).ap()
    ps_c = nc.alloc_psum_tensor("ps_c", [128, 640], F32).ap()
    ps_d = nc.alloc_psum_tensor("ps_d", [128, 512], F32).ap()
    ps_e = nc.alloc_psum_tensor("ps_e", [128, 512], F32).ap()

    def dma(queue, out, in_):
        S.op(queue, lambda e, o=out, i=in_: e.dma_start(out=o, in_=i),
             reads=[in_], writes=[out], kind="d")

    def mm(out, lhsT, rhs, start, stop):
        S.op("pe", lambda e, o=out, l=lhsT, r=rhs, s1_=start, s2_=stop:
             e.matmul(o, l, r, start=s1_, stop=s2_),
             reads=[lhsT, rhs] + ([] if start else [out]), writes=[out])

    def transpose(out_ps, in_sb, idn):
        S.op("pe", lambda e, o=out_ps, i=in_sb, d_=idn: e.transpose(o, i, d_),
             reads=[in_sb, idn], writes=[out_ps])

    def act(out, in_, func, bias=0.0, scale=1.0, accum=None):
        S.op("act", lambda e, o=out, i=in_, f=func, b=bias, s_=scale, a=accum:
             e.activation(o, i, f, bias=b, scale=s_, accum_out=a),
             reads=[in_, bias, scale], writes=[out, accum])

    def ts(out, in_, v1, v2, op1, op2=None):
        if op2 is None:
            S.op("dve", lambda e, o=out, i=in_, a=v1:
                 e.tensor_scalar(o, i, a, None, op1),
                 reads=[in_, v1], writes=[out])
        else:
            S.op("dve", lambda e, o=out, i=in_, a=v1, b=v2:
                 e.tensor_scalar(o, i, a, b, op1, op2),
                 reads=[in_, v1, v2], writes=[out])

    def tt(out, a, b, op, eng="dve"):
        S.op(eng, lambda e, o=out, x=a, y=b: e.tensor_tensor(o, x, y, op),
             reads=[a, b], writes=[out])

    def stt(out, in0, scalar, in1, op0, op1):
        S.op("dve", lambda e, o=out, i0=in0, s_=scalar, i1=in1:
             e.scalar_tensor_tensor(o, i0, s_, i1, op0, op1),
             reads=[in0, scalar, in1], writes=[out])

    def copy(out, in_, eng="dve"):
        if eng == "act":
            S.op(eng, lambda e, o=out, i=in_: e.copy(o, i),
                 reads=[in_], writes=[out])
        else:
            S.op(eng, lambda e, o=out, i=in_: e.tensor_copy(o, i),
                 reads=[in_], writes=[out])

    def recip(out, in_):
        S.op("dve", lambda e, o=out, i=in_: e.reciprocal(o, i),
             reads=[in_], writes=[out])

    def red(out, in_, op, absv=False, neg=False):
        S.op("dve", lambda e, o=out, i=in_, a=absv, n=neg:
             e.tensor_reduce(o, i, AX.X, op, apply_absolute_value=a, negate=n),
             reads=[in_], writes=[out])

    def shuffle_pairs(out, in_):
        mask = []
        for i in range(0, 32, 2):
            mask += [i + 1, i]
        S.op("dve", lambda e, o=out, i=in_, m=mask: e.stream_shuffle(o, i, m),
             reads=[in_], writes=[out])

    def memset(buf, v, eng="dve"):
        S.op(eng, lambda e, o=buf, c=v: e.memset(o, c), reads=[], writes=[buf])

    def quant_chain(src, out_q, scale_ap, tmp1, tmp2, ti32, ceng="dve"):
        act(tmp1, src, AF.Identity, bias=magicv, scale=scale_ap)
        ts(tmp2, tmp1, MAGIC, -QMAX - 1.0, AL.subtract, AL.max)
        ts(tmp1, tmp2, QMAX, None, AL.min)
        copy(ti32, tmp1, eng=ceng)
        copy(out_q, ti32, eng=ceng)

    def ln_stats(x_tile, out_ln, n_elem, sq_scratch):
        act(sq_scratch, x_tile, AF.Square, accum=sm["ssq"])
        red(sm["sumc"], x_tile, AL.add)
        ts(sm["mc"], sm["sumc"], 1.0 / n_elem, None, AL.mult)
        ts(sm["msqc"], sm["ssq"], 1.0 / n_elem, None, AL.mult)
        tt(sm["m2c"], sm["mc"], sm["mc"], AL.mult)
        tt(sm["varc"], sm["msqc"], sm["m2c"], AL.subtract)
        act(sm["svc"], sm["varc"], AF.Sqrt, bias=epsv)
        recip(sm["rstdc"], sm["svc"])
        tt(sm["nmrc"], sm["mc"], sm["rstdc"], AL.mult)
        ts(sm["nmrc"], sm["nmrc"], -1.0, None, AL.mult)
        act(out_ln, x_tile, AF.Identity, bias=sm["nmrc"], scale=sm["rstdc"])

    def collective(cc_in_ap, cc_out_ap):
        S.op("pool", lambda e, i=cc_in_ap, o=cc_out_ap: e.collective_compute(
            "AllReduce", AL.max, replica_groups=[list(range(NC))],
            ins=[i.opt()], outs=[o.opt()]),
            reads=[cc_in_ap], writes=[cc_out_ap], kind="x")

    dma("sync", ident, ident_in)
    copy(identb, ident)
    dma("sync", maskDT, maskDT_in)
    dma("sync", maskUT, maskUT_in)
    memset(ones1, 1.0)
    memset(magicv, MAGIC)
    memset(epsv, 1e-5)
    memset(neg20v, -20.0)
    memset(z8, 0.0)
    dma("sync", onesel, bsel_in)
    dma("sync", clipx_sb, clip_x[0:1])
    dma("sync", clipdn_sb, clip_dn[0:1])
    act(s1["sigx"], clipx_sb, AF.Sigmoid)
    act(s1["sigd"], clipdn_sb, AF.Sigmoid)

    p0 = Ph("p0")
    wmodb = [p0.t("wmodb0", [128, 6 * D])] * 2
    c_sb = p0.t("c_sb", [BPC, D])
    siluc = p0.t("siluc", [BPC, D])
    siluc_fm = p0.t("siluc_fm", [128, 2 * KCH])
    mod_sb = p0.t("mod_sb", [BPC, 6 * D])
    vr1 = p0.t("vr1", [BPC, D])
    vr2 = p0.t("vr2", [BPC, D])
    vr3 = p0.t("vr3", [BPC, D])
    ga_row = p0.t("ga_row", [BPC, D])
    gb_row = p0.t("gb_row", [BPC, D])

    dma("sync", c_sb, c_in)
    act(siluc, c_sb, AF.Silu)
    for k in range(KCH):
        transpose(ps_e[0:128, 0:2], siluc[:, k * 128:(k + 1) * 128], ident[0:2, 0:2])
        copy(siluc_fm[:, 2 * k:2 * k + 2], ps_e[0:128, 0:2], eng="act")
    for k in range(KCH):
        wb = wmodb[k % 2]
        dma("sync", wb, wmod[k * 128:(k + 1) * 128, :])
        for j in range(12):
            mm(ps_a[0:BPC, 0:512], siluc_fm[:, 2 * k:2 * k + 2],
               wb[:, j * 512:(j + 1) * 512], start=True, stop=True)
            if k == 0:
                copy(mod_sb[:, j * 512:(j + 1) * 512], ps_a[0:BPC, 0:512],
                     eng="act")
            else:
                tt(mod_sb[:, j * 512:(j + 1) * 512], ps_a[0:BPC, 0:512],
                   mod_sb[:, j * 512:(j + 1) * 512], AL.add)

    def clamp_g(dst, col0):
        ts(vr1, mod_sb[:, col0:col0 + D], 0.0, 2.0, AL.is_ge, AL.mult)
        ts(vr2, vr1, -1.0, None, AL.add)
        act(vr3, mod_sb[:, col0:col0 + D], AF.Abs)
        ts(vr3, vr3, 0.25, None, AL.max)
        tt(dst, vr2, vr3, AL.mult)

    clamp_g(ga_row, 2 * D)
    clamp_g(gb_row, 5 * D)
    ts(cb_row, mod_sb[:, 4 * D:5 * D], 1.0, None, AL.add)
    copy(sb_row, mod_sb[:, 3 * D:4 * D])
    ts(vr1, mod_sb[:, 1 * D:2 * D], 1.0, None, AL.add)

    def fm_pack(dst_tab, src_rows):
        for k in range(KCH):
            transpose(ps_e[0:128, 0:2], src_rows[:, k * 128:(k + 1) * 128],
                      ident[0:2, 0:2])
            copy(dst_tab[:, 2 * k:2 * k + 2], ps_e[0:128, 0:2], eng="act")

    fm_pack(camod_fm, vr1)
    fm_pack(sa_fm, mod_sb[:, 0:D])
    fm_pack(ga_tab, ga_row)
    fm_pack(gb_tab, gb_row)

    p1 = p0
    wt = [p1.t(f"wt{i}", [128, DI]) for i in range(2)]
    qt1 = [p1.t(f"qt1_{i}", [128, DI]) for i in range(2)]
    qt2 = [p1.t("qt2", [128, DI])] * 2
    qi = [p1.t("qi", [128, DI], I32)] * 2
    qf = [p1.t(f"qf{i}", [128, DI]) for i in range(2)]
    tq = [p1.t(f"tq{i}", [128, NBI * 128], BF16) for i in range(2)]

    def prep_scales(w_ap, n_tiles, tab, tile_cols):
        for it in range(n_tiles):
            b_ = wt[it % 2]
            dma("sync", b_[:, 0:tile_cols], w_ap[it * 128:(it + 1) * 128, :])
            red(tab[:, it:it + 1], b_[:, 0:tile_cols], AL.max, absv=True)

    prep_scales(w_gate, NBI, swg_tab, D)
    prep_scales(w_up, NBI, swu_tab, D)
    prep_scales(w_down, KCH, swd_tab, DI)
    for tab, rwt, n_ in ((swg_tab, rwg_tab, NBI), (swu_tab, rwu_tab, NBI),
                         (swd_tab, rwd_tab, KCH)):
        ts(tab[:, 0:n_], tab[:, 0:n_], 1.0 / QMAX, None, AL.mult)
        recip(rwt[:, 0:n_], tab[:, 0:n_])

    def prep_weight(w_ap, n_tiles, n_blocks, rwt, sp, tile_cols):
        for it in range(n_tiles):
            p_ = it % 2
            b_ = wt[p_]
            dma("sync", b_[:, 0:tile_cols], w_ap[it * 128:(it + 1) * 128, :])
            quant_chain(b_[:, 0:tile_cols], qf[p_][:, 0:tile_cols],
                        rwt[:, it:it + 1], qt1[p_][:, 0:tile_cols],
                        qt2[p_][:, 0:tile_cols], qi[p_][:, 0:tile_cols])
            tb = tq[p_]
            for c in range(n_blocks):
                pst = (ps_c if c % 2 == 0 else ps_d)[:, 0:128]
                transpose(pst, qf[p_][:, c * 128:(c + 1) * 128], ident)
                copy(tb[:, c * 128:(c + 1) * 128], pst, eng="act")
            dma("pool", sp[it], tb[:, 0:n_blocks * 128])

    prep_weight(w_gate, NBI, KCH, rwg_tab, wgq_sp, D)
    prep_weight(w_up, NBI, KCH, rwu_tab, wuq_sp, D)
    prep_weight(w_down, KCH, NBI, rwd_tab, wdq_sp, DI)
    S.barrier()

    p2 = Ph("p2")
    wtmp2b = [p2.t(f"wtmp2b{i}", [128, 3 * D]) for i in range(2)]
    wqh = p2.t("wqh", [128, KCH * 3 * D], BF16)
    wql = p2.t("wql", [128, KCH * 3 * D], BF16)
    x_t = [p2.t(f"x_t{i}", [128, D]) for i in range(2)]
    ln_t = p2.t("ln_t", [128, D])
    lnh = p2.t("lnh", [128, KCH * ST], BF16)
    lnl = p2.t("lnl", [128, KCH * ST], BF16)
    sqs = p2.t("sqs", [128, D])
    vfm = p2.t("vfm", [128, KCH * ST])
    vtmb = p2.t("vtmb", [128, H * 65])
    vth = p2.t("vth", [128, H * 65], BF16)
    vtl = p2.t("vtl", [128, H * 65], BF16)
    qkf = [p2.t(f"qkf{i}", [128, ST]) for i in range(2)]
    rqs = p2.t("rqs", [128, ST])
    rt1 = p2.t("rt1", [128, ST])
    rhib = [p2.t(f"rhib{i}", [128, ST], BF16) for i in range(2)]
    rlob = [p2.t(f"rlob{i}", [128, ST], BF16) for i in range(2)]
    tabc = p2.t("tabc", [128, ST])
    tabs_ = p2.t("tabs", [128, ST])

    memset(vtmb.rearrange("p (h c) -> p h c", c=65)[:, :, 64:65], 1.0)

    for b in range(BPC):
        for k in range(KCH):
            wtmp = wtmp2b[k % 2]
            dma("sync", wtmp, w_qkv[k * 128:(k + 1) * 128, :])
            if b == 0:
                for nb in range(NBQ):
                    mm(ps_e[:, 2 * nb:2 * nb + 2],
                       wtmp[:, nb * 128:(nb + 1) * 128], sa_fm[:, 2 * k:2 * k + 2],
                       start=(k == 0 and nb == 0),
                       stop=(k == KCH - 1 and nb == NBQ - 1))
            ts(wtmp, wtmp, camod_fm[:, 2 * k + b:2 * k + b + 1], None, AL.mult)
            copy(wqh[:, k * 3 * D:(k + 1) * 3 * D], wtmp)
            stt(wql[:, k * 3 * D:(k + 1) * 3 * D], wtmp, 1.0,
                wqh[:, k * 3 * D:(k + 1) * 3 * D], AL.mult, AL.subtract)
        if b == 0:
            copy(biasq, ps_e[:, 0:2 * NBQ], eng="act")
            ts(biasq_q, biasq[:, 0:16], 0.125, None, AL.mult)

        for st in range(NST):
            t0 = st * ST
            dma("sync", tabc, cosk_in[:, t0:t0 + ST])
            dma("sync", tabs_, sink_in[:, t0:t0 + ST])
            for t in range(4):
                xt_ = x_t[t % 2]
                dma("sync", xt_, x_in[b, t0 + t * 128:t0 + (t + 1) * 128, :])
                ln_stats(xt_, ln_t, D, sqs)
                for c in range(KCH):
                    pst = (ps_c if c % 2 == 0 else ps_d)[:, 0:128]
                    transpose(pst, ln_t[:, c * 128:(c + 1) * 128], ident)
                    act(lnh[:, c * ST + t * 128:c * ST + (t + 1) * 128], pst,
                        AF.Identity)
                    stt(lnl[:, c * ST + t * 128:c * ST + (t + 1) * 128], pst, 1.0,
                        lnh[:, c * ST + t * 128:c * ST + (t + 1) * 128],
                        AL.mult, AL.subtract)
            for nb in range(NBQ):
                ps = (ps_a if nb % 2 == 0 else ps_b)[:, 0:ST]
                passes = [(wqh, lnh), (wqh, lnl), (wql, lnh)]
                nmm = len(passes) * KCH
                i = 0
                for (wsrc, xsrc) in passes:
                    for k in range(KCH):
                        mm(ps, wsrc[:, k * 3 * D + nb * 128:k * 3 * D + (nb + 1) * 128],
                           xsrc[:, k * ST:(k + 1) * ST],
                           start=(i == 0), stop=(i == nmm - 1))
                        i += 1
                if nb >= 16:
                    c = nb - 16
                    act(vfm[:, c * ST:(c + 1) * ST], ps, AF.Identity,
                        bias=biasq[:, 2 * nb + b:2 * nb + b + 1])
                else:
                    qk = qkf[nb % 2]
                    if nb < 8:
                        act(qk, ps, AF.Identity, scale=0.125,
                            bias=biasq_q[:, 2 * nb + b:2 * nb + b + 1])
                    else:
                        act(qk, ps, AF.Identity,
                            bias=biasq[:, 2 * nb + b:2 * nb + b + 1])
                    shuffle_pairs(rqs, qk)
                    tt(rt1, qk, tabc, AL.mult)
                    tt(rqs, rqs, tabs_, AL.mult)
                    tt(rt1, rt1, rqs, AL.add)
                    hib, lob = rhib[nb % 2], rlob[nb % 2]
                    copy(hib, rt1)
                    stt(lob, rt1, 1.0, hib, AL.mult, AL.subtract)
                    c = nb if nb < 8 else nb - 8
                    sph, spl = (qh_sp, ql_sp) if nb < 8 else (kh_sp, kl_sp)
                    dma("pool", sph[b, c, :, t0:t0 + ST], hib)
                    dma("pool", spl[b, c, :, t0:t0 + ST], lob)
            for t in range(4):
                for c in range(KCH):
                    pst = (ps_c if c % 2 == 0 else ps_d)[:, 0:128]
                    transpose(pst, vfm[:, c * ST + t * 128:c * ST + (t + 1) * 128],
                              ident)
                    copy(vtmb[:, (2 * c) * 65:(2 * c) * 65 + 64], pst[:, 0:64],
                         eng="act")
                    copy(vtmb[:, (2 * c + 1) * 65:(2 * c + 1) * 65 + 64],
                         pst[:, 64:128], eng="act")
                copy(vth, vtmb)
                stt(vtl, vtmb, 1.0, vth, AL.mult, AL.subtract)
                dma("pool", v_sph[b, st * 4 + t], vth)
                dma("pool", v_spl[b, st * 4 + t], vtl)
    S.barrier()

    p3 = Ph("p3")
    wtmp2 = p3.t("wtmp2", [128, D])
    wouth = p3.t("wouth", [128, KCH * D], BF16)
    woutl = p3.t("woutl", [128, KCH * D], BF16)
    khr = [p3.t(f"khr{i}", [128, KCH * 3 * 128], BF16) for i in range(2)]
    klr = [p3.t(f"klr{i}", [128, KCH * 3 * 128], BF16) for i in range(2)]
    vtrh = [p3.t(f"vtrh{i}", [128, 3 * H * 65], BF16) for i in range(2)]
    vtrl = [p3.t(f"vtrl{i}", [128, 3 * H * 65], BF16) for i in range(2)]
    qbh = [p3.t(f"qbh{i}", [128, KCH * 128], BF16) for i in range(2)]
    qbl = [p3.t(f"qbl{i}", [128, KCH * 128], BF16) for i in range(2)]
    x_t2 = [p3.t(f"x_t2{i}", [128, D]) for i in range(2)]
    prf = [p3.t(f"prf{i}", [128, 640]) for i in range(4)]
    prth = [p3.t(f"prth{i}", [128, 640], BF16) for i in range(4)]
    prtl = [p3.t(f"prtl{i}", [128, 640], BF16) for i in range(4)]
    attn_tm = [p3.t(f"attn_tm{i}", [128, D]) for i in range(4)]
    attn_fmh = p3.t("attn_fmh", [128, KCH * ST], BF16)
    attn_fml = p3.t("attn_fml", [128, KCH * ST], BF16)
    projga = [p3.t(f"projga{i}", [128, ST]) for i in range(2)]
    x1_t = [p3.t(f"x1_t{i}", [128, D]) for i in range(4)]
    ln2b = p3.t("ln2b", [128, D])
    h2t = [p3.t(f"h2t{i}", [128, D]) for i in range(2)]
    cbrep = p3.t("cbrep", [128, D])
    sbrep = p3.t("sbrep", [128, D])

    for k in range(KCH):
        dma("sync", wtmp2, w_out[k * 128:(k + 1) * 128, :])
        copy(wouth[:, k * D:(k + 1) * D], wtmp2)
        stt(woutl[:, k * D:(k + 1) * D], wtmp2, 1.0,
            wouth[:, k * D:(k + 1) * D], AL.mult, AL.subtract)

    def ring_spans(kbs):
        spans = []
        j = 0
        n = len(kbs)
        while j < n:
            s0 = kbs[j] % RING
            ln_ = 1
            while (j + ln_ < n and (kbs[j + ln_] % RING) == s0 + ln_
                   and ln_ < 4 and (j + ln_) != 4):
                ln_ += 1
            spans.append((j, ln_, s0))
            j += ln_
        return spans

    for b in range(BPC):
        for half in range(2):
            mm(ps_d[0:128, 0:512], onesel[:, b * 128:(b + 1) * 128],
               cb_row[:, half * 512:(half + 1) * 512], start=True, stop=True)
            copy(cbrep[:, half * 512:(half + 1) * 512], ps_d[0:128, 0:512], eng="act")
            mm(ps_e[0:128, 0:512], onesel[:, b * 128:(b + 1) * 128],
               sb_row[:, half * 512:(half + 1) * 512], start=True, stop=True)
            copy(sbrep[:, half * 512:(half + 1) * 512], ps_e[0:128, 0:512], eng="act")
        def a2_loads(qt):
            pq = qt % 2
            slot = qt % RING
            for c_ in range(KCH):
                tsl = slice(qt * 128, (qt + 1) * 128)
                dma("sync", qbh[pq][:, c_ * 128:(c_ + 1) * 128],
                    qh_sp[b, c_, :, tsl])
                dma("sync", qbl[pq][:, c_ * 128:(c_ + 1) * 128],
                    ql_sp[b, c_, :, tsl])
                ro = c_ * 3 * 128 + (slot // 2) * 128
                dma("sync", khr[slot % 2][:, ro:ro + 128], kh_sp[b, c_, :, tsl])
                dma("sync", klr[slot % 2][:, ro:ro + 128], kl_sp[b, c_, :, tsl])
            vo0 = (slot // 2) * H * 65
            dma("sync", vtrh[slot % 2][:, vo0:vo0 + H * 65], v_sph[b, qt])
            dma("sync", vtrl[slot % 2][:, vo0:vo0 + H * 65], v_spl[b, qt])

        if _LIMIT_QT is None:
            a2_loads(0)
        for st in range(NST if _LIMIT_QT is None else 0):
            for tq in range(4):
                qt = st * 4 + tq
                pq = qt % 2
                slot = qt % RING
                if qt + 1 < NQT:
                    a2_loads(qt + 1)

                nk = min(qt, 4) + 1
                kbs = list(range(qt - nk + 1, qt + 1))
                sps = [ps_a, ps_b, ps_c]

                def scores_T(h):
                    c = h // 2
                    r0 = 64 * (h % 2)
                    ps = sps[h % 3]
                    passes = [(khr, qbh[pq]), (khr, qbl[pq]), (klr, qbh[pq])]
                    ops = []
                    for ip, (ksrc, qsrc) in enumerate(passes):
                        for j, kb_ in enumerate(kbs):
                            ops.append(("s", ip, j, kb_ % RING))
                    ops.append(("mD", nk - 1))
                    if nk == 5:
                        ops.append(("mU", 0))
                    reg = lambda j: 0 if j < 4 else 1
                    first_r, last_r = {}, {}
                    for i_, o_ in enumerate(ops):
                        r = reg(o_[2] if o_[0] == "s" else o_[1])
                        if r not in first_r:
                            first_r[r] = i_
                        last_r[r] = i_
                    out = []
                    for i_, o_ in enumerate(ops):
                        if o_[0] == "s":
                            _, ip, j, s0 = o_
                            ksrc, qsrc = passes[ip]
                            kt = ksrc[s0 % 2]
                            ko = c * 3 * 128 + (s0 // 2) * 128
                            out.append((ps[:, j * 128:(j + 1) * 128],
                                        kt[r0:r0 + 64, ko:ko + 128],
                                        qsrc[r0:r0 + 64, c * 128:(c + 1) * 128],
                                        first_r[reg(j)] == i_,
                                        last_r[reg(j)] == i_))
                        else:
                            j = o_[1]
                            mtile = maskDT if o_[0] == "mD" else maskUT
                            out.append((ps[:, j * 128:(j + 1) * 128],
                                        identb, mtile,
                                        first_r[reg(j)] == i_,
                                        last_r[reg(j)] == i_))
                    return out

                def tail_exp(h):
                    p4_ = h % 4
                    ps = sps[h % 3]
                    act(prth[p4_][:, 0:nk * 128], ps[:, 0:nk * 128], AF.Exp,
                        bias=neg20v)
                    act(prf[p4_][:, 0:nk * 128], ps[:, 0:nk * 128], AF.Exp,
                        bias=neg20v)
                    stt(prtl[p4_][:, 0:nk * 128], prf[p4_][:, 0:nk * 128], 1.0,
                        prth[p4_][:, 0:nk * 128], AL.mult, AL.subtract)

                def tail_pv(h):
                    p4_ = h % 4
                    pv = (ps_d if h % 2 == 0 else ps_e)[:, 0:65]
                    pvp = [(prth[p4_], vtrh), (prth[p4_], vtrl), (prtl[p4_], vtrh)]
                    nmm2 = 3 * nk
                    i2_ = 0
                    for (psrc, vsrc) in pvp:
                        for j, kb_ in enumerate(kbs):
                            sl_ = kb_ % RING
                            vo = (sl_ // 2) * H * 65 + h * 65
                            mm(pv, psrc[:, j * 128:(j + 1) * 128],
                               vsrc[sl_ % 2][:, vo:vo + 65],
                               start=(i2_ == 0), stop=(i2_ == nmm2 - 1))
                            i2_ += 1
                    rsv = sm["rs0" if h % 2 == 0 else "rs1"]
                    recip(rsv, pv[:, 64:65])
                    act(attn_tm[tq][:, h * 64:h * 64 + 64], pv[:, 0:64],
                        AF.Identity, scale=rsv)

                for hp in range(8):
                    se = scores_T(2 * hp)
                    so = scores_T(2 * hp + 1)
                    ni = max(len(se), len(so))
                    for i_ in range(ni):
                        if i_ < len(se):
                            mm(*se[i_])
                        if i_ < len(so):
                            mm(*so[i_])
                    tail_exp(2 * hp)
                    tail_exp(2 * hp + 1)
                    if hp > 0:
                        tail_pv(2 * hp - 2)
                        tail_pv(2 * hp - 1)
                tail_pv(14)
                tail_pv(15)
            for tq in range(4):
                for c in range(KCH):
                    pst = (ps_c if c % 2 == 0 else ps_d)[:, 0:128]
                    transpose(pst, attn_tm[tq][:, c * 128:(c + 1) * 128], ident)
                    col = c * ST + tq * 128
                    act(attn_fmh[:, col:col + 128], pst, AF.Identity)
                    stt(attn_fml[:, col:col + 128], pst, 1.0,
                        attn_fmh[:, col:col + 128], AL.mult, AL.subtract)
            for n in range(KCH):
                ps = (ps_a if n % 2 == 0 else ps_b)[:, 0:ST]
                passes2 = [(wouth, attn_fmh), (wouth, attn_fml), (woutl, attn_fmh)]
                nmm = 3 * KCH
                i_ = 0
                for (wsrc, xsrc) in passes2:
                    for k in range(KCH):
                        mm(ps, wsrc[:, k * D + n * 128:k * D + (n + 1) * 128],
                           xsrc[:, k * ST:(k + 1) * ST],
                           start=(i_ == 0), stop=(i_ == nmm - 1))
                        i_ += 1
                act(projga[n % 2], ps, AF.Identity,
                    scale=ga_tab[:, 2 * n + b:2 * n + b + 1])
                for t in range(4):
                    pst = (ps_c if t % 2 == 0 else ps_d)[:, 0:128]
                    transpose(pst, projga[n % 2][:, t * 128:(t + 1) * 128], ident)
                    copy(x1_t[t][:, n * 128:(n + 1) * 128], pst)
            for t in range(4):
                qt = st * 4 + t
                dma("sync", x_t2[t % 2], x_in[b, qt * 128:(qt + 1) * 128, :])
                tt(x1_t[t], x1_t[t], x_t2[t % 2], AL.add, eng="pool")
                dma("pool", x1_sp[b, qt], x1_t[t])
                ln_stats(x1_t[t], ln2b, D, x_t2[(t + 1) % 2])
                tt(h2t[t % 2], ln2b, cbrep, AL.mult, eng="pool")
                tt(h2t[t % 2], h2t[t % 2], sbrep, AL.add, eng="pool")
                red(maxc_a[:, b * NQT + qt:b * NQT + qt + 1], h2t[t % 2],
                    AL.max, absv=True)
                dma("pool", h2_sp[b, qt], h2t[t % 2])
    red(sm["mcol"], maxc_a, AL.max)
    transpose(ps_e[0:1, 0:128], sm["mcol"], ident)
    copy(sc128, ps_e[0:1, 0:128], eng="act")
    red(s1["m1l"], sc128, AL.max)
    copy(z8[:, 0:1], s1["m1l"])
    dma("pool", cc1_i, z8)
    collective(cc1_i, cc1_o)
    dma("pool", s1["m1g"], cc1_o[0:1, 0:1])
    tt(s1["sxm"], s1["sigx"], s1["m1g"], AL.mult)
    ts(s1["sxs"], s1["sxm"], 1.0 / QMAX, None, AL.mult)
    recip(s1["rx1"], s1["sxs"])
    mm(ps_e[0:128, 0:1], ones1, s1["rx1"], start=True, stop=True)
    copy(rxv, ps_e[0:128, 0:1], eng="act")
    mm(ps_e[0:128, 1:2], ones1, s1["sxs"], start=True, stop=True)
    copy(sm["rw"], ps_e[0:128, 1:2], eng="act")
    ts(sxg_tab, swg_tab, sm["rw"], None, AL.mult)
    ts(sxu_tab, swu_tab, sm["rw"], None, AL.mult)
    S.barrier()

    p4 = Ph("p4")
    wgq_sb = p4.t("wgq_sb", [128, NBI * KCH * 128], BF16)
    wuq_sb = p4.t("wuq_sb", [128, NBI * KCH * 128], BF16)
    h2b = [p4.t(f"h2b{i}", [128, D]) for i in range(2)]
    qb1 = [p4.t(f"qb1_{i}", [128, D]) for i in range(2)]
    qb2 = [p4.t(f"qb2_{i}", [128, D]) for i in range(2)]
    qbi = [p4.t(f"qbi{i}", [128, D], I32) for i in range(2)]
    qbf = [p4.t(f"qbf{i}", [128, D]) for i in range(2)]
    h2q = [p4.t(f"h2q{i}", [128, KCH * ST], BF16) for i in range(2)]
    silu_b = [p4.t(f"silu_b{i}", [128, ST]) for i in range(2)]
    hmid = [p4.t(f"hmid{i}", [128, ST]) for i in range(2)]

    for i_ in range(NBI):
        dma("sync", wgq_sb[:, i_ * KCH * 128:(i_ + 1) * KCH * 128], wgq_sp[i_])
        dma("sync", wuq_sb[:, i_ * KCH * 128:(i_ + 1) * KCH * 128], wuq_sp[i_])

    def b1_quant(b, st):
        sp_ = (b * NST + st) % 2
        for t in range(4):
            hb_ = h2b[t % 2]
            dma("sync", hb_, h2_sp[b, st * 4 + t])
            quant_chain(hb_, qbf[t % 2], rxv, qb1[t % 2], qb2[t % 2],
                        qbi[t % 2])
            for c in range(KCH):
                pst = (ps_a if c % 2 == 0 else ps_b)[:, 512:640]
                transpose(pst, qbf[t % 2][:, c * 128:(c + 1) * 128], ident)
                copy(h2q[sp_][:, c * ST + t * 128:c * ST + (t + 1) * 128],
                     pst, eng="act")

    def b1_mms(b, st):
        sp_ = (b * NST + st) % 2
        for ib in range(NBI):
            pg = (ps_a if ib % 2 == 0 else ps_b)[:, 0:ST]
            for k in range(KCH):
                mm(pg, wgq_sb[:, ib * D + k * 128:ib * D + (k + 1) * 128],
                   h2q[sp_][:, k * ST:(k + 1) * ST],
                   start=(k == 0), stop=(k == KCH - 1))
            act(silu_b[ib % 2], pg, AF.Silu, scale=sxg_tab[:, ib:ib + 1])
            pu = (ps_c if ib % 2 == 0 else ps_d)[:, 0:ST]
            for k in range(KCH):
                mm(pu, wuq_sb[:, ib * D + k * 128:ib * D + (k + 1) * 128],
                   h2q[sp_][:, k * ST:(k + 1) * ST],
                   start=(k == 0), stop=(k == KCH - 1))
            stt(hmid[ib % 2], pu, sxu_tab[:, ib:ib + 1], silu_b[ib % 2],
                AL.mult, AL.mult)
            red(maxc_b[:, (b * NST + st) * NBI + ib:
                       (b * NST + st) * NBI + ib + 1],
                hmid[ib % 2], AL.max, absv=True)
            dma("pool", hm_sp[b, ib, :, st * ST:(st + 1) * ST], hmid[ib % 2])

    prev = None
    for b in range(BPC):
        for st in range(NST):
            b1_quant(b, st)
            if prev is not None:
                b1_mms(*prev)
            prev = (b, st)
    b1_mms(*prev)
    red(sm["mcol"], maxc_b, AL.max)
    transpose(ps_e[0:1, 0:128], sm["mcol"], ident)
    copy(sc128, ps_e[0:1, 0:128], eng="act")
    red(s1["m2l"], sc128, AL.max)
    copy(z8[:, 0:1], s1["m2l"])
    dma("pool", cc2_i, z8)
    collective(cc2_i, cc2_o)
    dma("pool", s1["m2g"], cc2_o[0:1, 0:1])
    tt(s1["smm"], s1["sigd"], s1["m2g"], AL.mult)
    ts(s1["sms"], s1["smm"], 1.0 / QMAX, None, AL.mult)
    recip(s1["rm1"], s1["sms"])
    mm(ps_e[0:128, 0:1], ones1, s1["rm1"], start=True, stop=True)
    copy(rmv, ps_e[0:128, 0:1], eng="act")
    mm(ps_e[0:128, 1:2], ones1, s1["sms"], start=True, stop=True)
    copy(sm["rw"], ps_e[0:128, 1:2], eng="act")
    ts(tmp8, swd_tab, sm["rw"], None, AL.mult)
    for b in range(BPC):
        for hc in range(KCH):
            tt(sdgb_tab[:, 2 * hc + b:2 * hc + b + 1], tmp8[:, hc:hc + 1],
               gb_tab[:, 2 * hc + b:2 * hc + b + 1], AL.mult)
    S.barrier()

    p5 = Ph("p5")
    wdq_sb = p5.t("wdq_sb", [128, KCH * NBI * 128], BF16)
    hmin = [p5.t(f"hmin{i}", [128, ST]) for i in range(2)]
    qc1 = [p5.t(f"qc1_{i}", [128, ST]) for i in range(2)]
    qc2 = [p5.t(f"qc2_{i}", [128, ST]) for i in range(2)]
    qci = [p5.t(f"qci{i}", [128, ST], I32) for i in range(2)]
    hmq = [p5.t(f"hmq{i}", [128, NBI * ST], BF16) for i in range(2)]
    x1b = [p5.t(f"x1b{i}", [128, D]) for i in range(4)]
    dwn = [p5.t(f"dwn{i}", [128, ST]) for i in range(2)]
    xo = [p5.t(f"xo{i}", [128, D]) for i in range(4)]

    for c_ in range(KCH):
        dma("sync", wdq_sb[:, c_ * NBI * 128:(c_ + 1) * NBI * 128], wdq_sp[c_])
    def b2_quant(b, st):
        sp_ = (b * NST + st) % 2
        for ic in range(NBI):
            hm_ = hmin[ic % 2]
            dma("sync", hm_, hm_sp[b, ic, :, st * ST:(st + 1) * ST])
            quant_chain(hm_, hmq[sp_][:, ic * ST:(ic + 1) * ST], rmv,
                        qc1[ic % 2], qc2[ic % 2], qci[ic % 2])

    def b2_mms(b, st):
        sp_ = (b * NST + st) % 2
        for t in range(4):
            dma("sync", x1b[t], x1_sp[b, st * 4 + t])
        for hb in range(KCH):
            ps = (ps_a if hb % 2 == 0 else ps_b)[:, 0:ST]
            for ic in range(NBI):
                mm(ps, wdq_sb[:, hb * DI + ic * 128:hb * DI + (ic + 1) * 128],
                   hmq[sp_][:, ic * ST:(ic + 1) * ST],
                   start=(ic == 0), stop=(ic == NBI - 1))
            act(dwn[hb % 2], ps, AF.Identity,
                scale=sdgb_tab[:, 2 * hb + b:2 * hb + b + 1])
            for t in range(4):
                pst = (ps_c if t % 2 == 0 else ps_d)[:, 0:128]
                transpose(pst, dwn[hb % 2][:, t * 128:(t + 1) * 128], ident)
                tt(xo[t][:, hb * 128:(hb + 1) * 128], pst,
                   x1b[t][:, hb * 128:(hb + 1) * 128], AL.add)
        for t in range(4):
            dma("pool", y_out[b, st * ST + t * 128:st * ST + (t + 1) * 128, :],
                xo[t])

    prev = None
    for b in range(BPC):
        for st in range(NST):
            b2_quant(b, st)
            if prev is not None:
                b2_mms(*prev)
            prev = (b, st)
    b2_mms(*prev)
    S.barrier()

    S.finalize()
    from contextlib import ExitStack
    with ExitStack() as stack:
        sems = {}
        for e in ENGINES:
            for k in ("c", "x"):
                sems[(e, k)] = stack.enter_context(nc.semaphore(f"sem_{e}_{k}"))
            for ln_ in range(S.NLANES):
                sems[(e, "d", ln_)] = stack.enter_context(
                    nc.semaphore(f"sem_{e}_d{ln_}"))
        block = stack.enter_context(nc.Block())

        def run_engine(handle, ename):
            for o in S.q[ename]:
                for (key, v) in o.waits:
                    handle.wait_ge(sems[key], v)
                instr = o.fn(handle)
                if o.inc:
                    if o.kind == "d":
                        instr.then_inc(sems[(ename, "d", o.lane)], 16)
                    else:
                        instr.then_inc(sems[(ename, o.kind)], 1)
            if ename == "pool":
                for ln_ in range(S.NLANES):
                    fv = S.lane_final["pool"][ln_]
                    if fv > 0:
                        handle.wait_ge(sems[("pool", "d", ln_)], fv)

        @block.tensor
        def _(engh):
            run_engine(engh, "pe")

        @block.vector
        def _(engh):
            run_engine(engh, "dve")

        @block.scalar
        def _(engh):
            run_engine(engh, "act")

        @block.gpsimd
        def _(engh):
            run_engine(engh, "pool")

        @block.sync
        def _(engh):
            run_engine(engh, "sync")

    return nc



def _host_constants():
    import ml_dtypes
    ident = np.eye(128, dtype=np.float32)
    kk, qq = np.meshgrid(np.arange(128), np.arange(128), indexing="ij")
    maskDT = np.where(qq >= kk, 0.0, -1e30).astype(ml_dtypes.bfloat16)
    maskUT = np.where(kk > qq, 0.0, -1e30).astype(ml_dtypes.bfloat16)
    inv = (1.0 / (np.float32(10000.0) **
                  (np.arange(0, HD, 2, dtype=np.float32) / np.float32(HD))))
    inv = inv.astype(np.float32)
    tpos = np.arange(L, dtype=np.float32)
    freqs = (inv[:, None] * tpos[None, :]).astype(np.float32)
    cos1 = np.cos(freqs).astype(np.float32)
    sin1 = np.sin(freqs).astype(np.float32)
    cos64 = np.repeat(cos1, 2, axis=0)
    sin64 = np.repeat(sin1, 2, axis=0)
    sgn = np.where(np.arange(HD) % 2 == 0, -1.0, 1.0).astype(np.float32)[:, None]
    sinS = (sin64 * sgn).astype(np.float32)
    cos128 = np.concatenate([cos64, cos64], axis=0).astype(np.float32)
    sinS128 = np.concatenate([sinS, sinS], axis=0).astype(np.float32)
    bsel = np.zeros((BPC, BPC * 128), dtype=np.float32)
    for b_ in range(BPC):
        bsel[b_, b_ * 128:(b_ + 1) * 128] = 1.0
    return dict(ident_in=ident, maskDT_in=maskDT, maskUT_in=maskUT,
                cosk_in=cos128, sink_in=sinS128, bsel_in=bsel)


_NC_CACHE = None


def kernel(L0_attn_x, L0_attn_w_qkv, L0_attn_w_out, L0_ffn_clip_x,
           L0_ffn_clip_down, L0_ffn_w_gate, L0_ffn_w_up, L0_ffn_w_down,
           L0_c, L0_wmod):
    global _NC_CACHE
    if _NC_CACHE is None:
        _NC_CACHE = build_program()
    nc = _NC_CACHE
    consts = _host_constants()
    x = np.ascontiguousarray(np.asarray(L0_attn_x, dtype=np.float32))
    cvec = np.ascontiguousarray(np.asarray(L0_c, dtype=np.float32))
    shared = dict(
        w_qkv=np.ascontiguousarray(np.asarray(L0_attn_w_qkv, np.float32)),
        w_out=np.ascontiguousarray(np.asarray(L0_attn_w_out, np.float32)),
        wmod=np.ascontiguousarray(np.asarray(L0_wmod, np.float32)),
        w_gate=np.ascontiguousarray(np.asarray(L0_ffn_w_gate, np.float32)),
        w_up=np.ascontiguousarray(np.asarray(L0_ffn_w_up, np.float32)),
        w_down=np.ascontiguousarray(np.asarray(L0_ffn_w_down, np.float32)),
        clip_x=np.ascontiguousarray(np.asarray(L0_ffn_clip_x, np.float32)),
        clip_dn=np.ascontiguousarray(np.asarray(L0_ffn_clip_down, np.float32)),
        **consts,
    )
    in_maps = []
    for c in range(NC):
        m = dict(shared)
        m["x_in"] = np.ascontiguousarray(x[BPC * c:BPC * (c + 1)])
        m["c_in"] = np.ascontiguousarray(cvec[BPC * c:BPC * (c + 1)])
        in_maps.append(m)
    res = run_bass_kernel_spmd(nc, in_maps, core_ids=list(range(NC)))
    return np.concatenate([np.asarray(r["y_out"]) for r in res.results], axis=0)


if __name__ == "__main__":
    import time
    t0 = time.time()
    nc = build_program()
    n = sum(1 for _ in nc.all_instructions())
    print(f"built ok in {time.time() - t0:.1f}s, {n} instructions")
